# version 05
# baseline (speedup 1.0000x reference)
import numpy as np
import concourse.bass as bass
import concourse.mybir as mybir
from concourse.alu_op_type import AluOpType
from concourse.bass_utils import run_bass_kernel_spmd
from contextlib import ExitStack

F32 = mybir.dt.float32
BF16 = mybir.dt.bfloat16
U32 = mybir.dt.uint32
ACT = mybir.ActivationFunctionType
AL = AluOpType

T = 2048
D = 1024
H = 16
DH = 64
HID = 2816
E = 8
NR = 1369
PRE = T - NR
NT = T // 128
NC = D // 128
HT = HID // 128
M_SOFT = 32.0

ENGINES = ("tensor", "vector", "scalar", "sync", "gpsimd")


class Plan:

    def __init__(self, nc, sem_names):
        self.nc = nc
        self.prog = {e: [] for e in ENGINES}
        self.count = {s: 0 for s in sem_names}
        self.handles = {}
        self.waited = {e: {s: 0 for s in sem_names} for e in ENGINES}

    def wait(self, eng, sem, val=None):
        v = self.count[sem] if val is None else val
        if v is None or v <= 0 or v <= self.waited[eng][sem]:
            return
        self.waited[eng][sem] = v
        h = self.handles[sem]
        self.prog[eng].append(lambda En, h=h, v=v: En.wait_ge(h, v))

    def run(self, eng, fn, inc=None, n=1):
        if inc is None:
            self.prog[eng].append(fn)
            return self.count.get("_", 0)
        h = self.handles[inc]
        self.count[inc] += n
        self.prog[eng].append(lambda En, fn=fn, h=h, n=n: fn(En).then_inc(h, n))
        return self.count[inc]

    def tok(self, sem):
        return self.count[sem]

    def drain(self, eng):
        self.prog[eng].append(lambda En: En.drain())

    def runD(self, eng, fn, inc, n=1):
        self.prog[eng].append(fn)
        h = self.handles[inc]
        self.count[inc] += n
        self.prog[eng].append(lambda En, h=h, n=n: En.drain().then_inc(h, n))
        return self.count[inc]


class Buf:

    def __init__(self, arena, off_units, width_elems, dtype):
        self.arena = arena
        self.off = off_units
        self.width = width_elems
        self.dtype = dtype
        self.esz = 1 if dtype == BF16 else 2

    def ap(self, c0=0, c1=None, r0=0, r1=128):
        if c1 is None:
            c1 = self.width
        a = self.arena[r0:r1, self.off + c0 * self.esz: self.off + c1 * self.esz]
        if self.dtype != BF16:
            a = a.bitcast(self.dtype)
        return a


def build_nc():
    nc = bass.Bass("TRN2", target_bir_lowering=False)
    x_d = nc.dram_tensor("x", [T, D], F32, kind="ExternalInput").ap()
    sin_d = nc.dram_tensor("sin_t", [128, NR], F32, kind="ExternalInput").ap()
    cos_d = nc.dram_tensor("cos_t", [128, NR], F32, kind="ExternalInput").ap()
    wqkv_d = nc.dram_tensor("w_qkv", [D, 3 * D], F32, kind="ExternalInput").ap()
    bqkv_d = nc.dram_tensor("b_qkv_pp", [128, 24], F32, kind="ExternalInput").ap()
    bv_d = nc.dram_tensor("b_v_bc", [128, D], F32, kind="ExternalInput").ap()
    wproj_d = nc.dram_tensor("w_proj", [D, D], F32, kind="ExternalInput").ap()
    bproj_d = nc.dram_tensor("b_proj_bc", [128, D], F32, kind="ExternalInput").ap()
    nw_d = nc.dram_tensor("nw_bc", [128, D], F32, kind="ExternalInput").ap()
    rt_d = nc.dram_tensor("rt4", [NC, 128, E], F32, kind="ExternalInput").ap()
    gup_d = nc.dram_tensor("gup", [E, 2 * HT, 128, D], F32, kind="ExternalInput").ap()
    down_d = nc.dram_tensor("down", [E, HT, 128, D], F32, kind="ExternalInput").ap()
    alpha_d = nc.dram_tensor("alpha_t", [128, 1], F32, kind="ExternalInput").ap()
    y_d = nc.dram_tensor("y", [T, D], F32, kind="ExternalOutput").ap()

    with ExitStack() as ctx:
        en = ctx.enter_context
        arena_h = en(nc.sbuf_tensor("arena", [128, 106240], BF16))
        arena = arena_h[:]
        PA = en(nc.psum_tensor("PA", [128, 2048], F32))
        PB = en(nc.psum_tensor("PB", [128, 2048], F32))

        sems = ["t", "v", "sc", "gd", "sd"]
        p = Plan(nc, sems)
        for s in sems:
            p.handles[s] = en(nc.semaphore(s))
        block = en(nc.Block())

        off = [0]

        def mk(width_elems, dtype, align=16):
            units = width_elems * (1 if dtype == BF16 else 2)
            off[0] = (off[0] + align - 1) // align * align
            b = Buf(arena, off[0], width_elems, dtype)
            off[0] += units
            return b

        ident = mk(128, BF16)
        R2 = mk(128, BF16)
        io8 = mk(E, F32)
        alpha_s = mk(1, F32)
        b_qkv_pp = mk(24, F32)
        r_col = mk(NT, F32)
        mr_col = mk(NT, F32)
        rr2_col = mk(NT, F32)
        mrr2_col = mk(NT, F32)
        tcol = mk(16, F32)
        lg_s = mk(E, F32)
        mx8 = mk(E, F32)
        mi8 = mk(E, U32)
        idxf = mk(2, F32)
        w12 = mk(2, F32)
        m1m2 = mk(2 * E, F32)
        c_coef = mk(128, F32)
        colidx = mk(128, F32)
        pidx = mk(4, F32)
        scrA = mk(128, F32)
        rec4 = mk(4, F32)
        negm = mk(1, F32)
        b_v_bc = mk(D, F32)
        b_proj_bc = mk(D, F32)
        nw_bc = mk(D, F32)
        rt_sb = mk(NC * E, BF16)

        REG1 = mk(16384, BF16, 256)
        REG2 = mk(16384, BF16, 256)
        REG3 = mk(16384, BF16, 256)
        REG4 = mk(16640, BF16, 256)
        REG5 = mk(16384, BF16, 256)
        REG6 = mk(16384, BF16, 256)
        assert off[0] <= 106240, off[0]

        xn_fm, O_tm = REG1, REG1
        q_fm, s_tm = REG2, REG2
        k_fm, moe = REG3, REG3
        v_tm, O_fm = REG4, REG4
        normed_fm = REG5

        def r6(off_u, width_elems, dtype):
            return Buf(arena, REG6.off + off_u, width_elems, dtype)

        xld = [r6(0, D, F32), r6(2048, D, F32)]
        sqb = r6(4096, D, F32)
        xn_stage = r6(6144, D, BF16)
        normed_pre = r6(7168, D, F32)
        normed_stage = r6(9216, D, BF16)
        sin_sb = r6(10256, NR, F32)
        cos_sb = r6(13024, NR, F32)
        wq_buf = [r6(0, 4096, BF16), r6(4096, 4096, BF16)]
        rope_s1 = r6(8192, 512, F32)
        rope_s2 = r6(9232, 512, F32)
        pT = [r6(0, 1024, BF16), r6(1024, 1024, BF16), r6(2048, 1024, BF16), r6(3072, 1024, BF16)]
        wproj_buf = r6(4096, NC * D, BF16)
        gup_buf = [r6(i * 1024, D, BF16) for i in range(4)]
        down_buf = [r6(4096 + i * 1024, D, BF16) for i in range(11)]
        silu_st = [r6(15360, 512, BF16), r6(15872, 512, BF16)]
        fx = [r6(0, D, F32), r6(2048, D, F32)]
        fscr1 = r6(4096, D, F32)
        fscr2 = r6(6144, D, F32)
        fout = [r6(8192, D, F32), r6(10240, D, F32)]

        for dst, src in [(b_qkv_pp.ap(), bqkv_d), (nw_bc.ap(), nw_d),
                         (b_v_bc.ap(), bv_d), (b_proj_bc.ap(), bproj_d),
                         (alpha_s.ap(), alpha_d), (sin_sb.ap(), sin_d),
                         (cos_sb.ap(), cos_d)]:
            init_sd = p.run("sync", lambda En, d=dst, s=src: En.dma_start(out=d, in_=s), inc="sd", n=16)

        p.run("gpsimd", lambda En: En.iota(io8.ap(), [[1, E]], channel_multiplier=0,
                                           allow_small_or_imprecise_dtypes=True), inc="gd", n=1)
        p.run("gpsimd", lambda En: En.iota(colidx.ap(), [[1, 128]], channel_multiplier=0,
                                           allow_small_or_imprecise_dtypes=True), inc="gd", n=1)
        p.run("gpsimd", lambda En: En.iota(pidx.ap(0, 1), [[0, 1]], channel_multiplier=1,
                                           allow_small_or_imprecise_dtypes=True), inc="gd", n=1)
        vone = v_tm.ap().rearrange("p (t h j) -> p t h j", t=NT, h=H)[:, :, :, 64:65]
        p.run("gpsimd", lambda En: En.memset(vone, 1.0), inc="gd", n=1)
        p.run("gpsimd", lambda En: En.iota(
            scrA.ap().rearrange("p (z y x) -> p z y x", z=2, y=2), [[0, 2], [2, 2], [0, 32]],
            base=-1, channel_multiplier=0, allow_small_or_imprecise_dtypes=True), inc="gd", n=1)
        init_gd = p.run("gpsimd", lambda En: En.dma_start(
            out=rt_sb.ap().rearrange("p (c e) -> p c e", c=NC),
            in_=rt_d.rearrange("c p e -> p c e")), inc="gd", n=16)

        p.wait("sync", "sd", init_sd)
        p.wait("gpsimd", "gd", init_gd)
        p.wait("vector", "gd", init_gd)
        p.wait("vector", "sd", init_sd)
        p.run("vector", lambda En: En.tensor_scalar(ident.ap(), colidx.ap(), pidx.ap(0, 1), None,
                                                    op0=AL.is_equal))
        p.run("vector", lambda En: En.tensor_scalar(pidx.ap(1, 2), pidx.ap(0, 1), 32.0, None, op0=AL.add))
        p.run("vector", lambda En: En.tensor_scalar(pidx.ap(2, 3), pidx.ap(0, 1), -32.0, None, op0=AL.add))
        p.run("vector", lambda En: En.tensor_scalar(tcol.ap(0, 1), pidx.ap(0, 1), 64.0, None, op0=AL.is_ge))
        p.drain("vector")
        p.run("vector", lambda En: En.tensor_scalar(tcol.ap(1, 2), tcol.ap(0, 1), -64.0, None, op0=AL.mult))
        p.drain("vector")
        p.run("vector", lambda En: En.tensor_tensor(tcol.ap(1, 2), pidx.ap(0, 1), tcol.ap(1, 2), op=AL.add))
        p.drain("vector")
        p.run("vector", lambda En: En.tensor_scalar(tcol.ap(2, 3), tcol.ap(1, 2), 32.0, None, op0=AL.is_lt))
        p.drain("vector")
        p.run("vector", lambda En: En.tensor_scalar(tcol.ap(3, 4), tcol.ap(2, 3), -1.0, 1.0,
                                                    op0=AL.mult, op1=AL.add))
        p.drain("vector")
        p.run("vector", lambda En: En.tensor_scalar(c_coef.ap(), colidx.ap(), pidx.ap(1, 2), None,
                                                    op0=AL.is_equal))
        p.run("vector", lambda En: En.tensor_scalar(colidx.ap(), colidx.ap(), pidx.ap(2, 3), None,
                                                    op0=AL.is_equal))
        p.drain("vector")
        p.run("vector", lambda En: En.tensor_scalar(c_coef.ap(), c_coef.ap(), tcol.ap(2, 3), None,
                                                    op0=AL.mult))
        p.run("vector", lambda En: En.tensor_scalar(colidx.ap(), colidx.ap(), tcol.ap(3, 4), None,
                                                    op0=AL.mult))
        p.drain("vector")
        p.run("vector", lambda En: En.tensor_tensor(c_coef.ap(), c_coef.ap(), colidx.ap(), op=AL.add))
        p.drain("vector")
        p.run("vector", lambda En: En.tensor_tensor(R2.ap(), c_coef.ap(), scrA.ap(), op=AL.mult))
        p.run("vector", lambda En: En.memset(negm.ap(), -M_SOFT))
        p.run("vector", lambda En: En.tensor_scalar_min(alpha_s.ap(), alpha_s.ap(), 2.0))
        consts_v = p.run("vector", lambda En: En.tensor_scalar_max(alpha_s.ap(), alpha_s.ap(), -0.5),
                         inc="v", n=1)

        ln_v_done = [0] * NT
        ln_sc_done = [0] * NT
        evac_v = {}
        stage_t = 0
        sq_v = [0] * NT
        ln_mm_t = 0

        for tt in range(NT):
            par = tt % 2
            if tt >= 2:
                p.wait("sync", "v", ln_v_done[tt - 2])
                p.wait("sync", "sc", ln_sc_done[tt - 2])
            x_tok = p.run("sync", lambda En, tt=tt, par=par: En.dma_start(
                out=xld[par].ap(), in_=x_d[tt * 128:(tt + 1) * 128, :]), inc="sd", n=16)

            p.wait("scalar", "sd", x_tok)
            if tt >= 1:
                p.wait("scalar", "v", sq_v[tt - 1])
            ln_sc_done[tt] = p.run("scalar", lambda En, par=par: En.activation(
                sqb.ap(), xld[par].ap(), ACT.Square), inc="sc", n=1)

            p.wait("vector", "sd", x_tok)
            p.wait("vector", "sc", ln_sc_done[tt])
            p.run("vector", lambda En, par=par: En.reduce_sum(tcol.ap(0, 1), xld[par].ap(),
                                                              axis=mybir.AxisListType.X))
            sq_v[tt] = p.run("vector", lambda En: En.reduce_sum(tcol.ap(1, 2), sqb.ap(),
                                                                axis=mybir.AxisListType.X), inc="v", n=1)
            p.drain("vector")
            p.run("vector", lambda En: En.tensor_scalar(tcol.ap(2, 3), tcol.ap(0, 1), 1.0 / D, None,
                                                        op0=AL.mult))
            p.run("vector", lambda En: En.tensor_scalar(tcol.ap(3, 4), tcol.ap(1, 2), 1.0 / D, None,
                                                        op0=AL.mult))
            p.drain("vector")
            p.run("vector", lambda En: En.tensor_tensor(tcol.ap(4, 5), tcol.ap(2, 3), tcol.ap(2, 3),
                                                        op=AL.mult))
            p.drain("vector")
            p.run("vector", lambda En: En.tensor_tensor(tcol.ap(3, 4), tcol.ap(3, 4), tcol.ap(4, 5),
                                                        op=AL.subtract))
            p.drain("vector")
            p.run("vector", lambda En: En.tensor_scalar(tcol.ap(4, 5), tcol.ap(3, 4), 1e-5, None,
                                                        op0=AL.add))
            p.drain("vector")
            p.run("vector", lambda En: En.reciprocal(tcol.ap(5, 6), tcol.ap(4, 5)))
            p.drain("vector")
            p.run("vector", lambda En: En.tensor_tensor(tcol.ap(6, 7), tcol.ap(3, 4), tcol.ap(5, 6),
                                                        op=AL.mult))
            p.drain("vector")
            p.run("vector", lambda En: En.tensor_scalar(tcol.ap(6, 7), tcol.ap(6, 7), 1e-6, None,
                                                        op0=AL.add))
            p.drain("vector")
            rv2_tok = p.runD("vector", lambda En: En.reciprocal(tcol.ap(7, 8), tcol.ap(6, 7)),
                             inc="v", n=1)
            p.wait("scalar", "v", rv2_tok)
            p.run("scalar", lambda En, tt=tt: En.activation(
                r_col.ap(tt, tt + 1), tcol.ap(5, 6), ACT.Sqrt))
            p.drain("scalar")
            r2_tok = p.runD("scalar", lambda En: En.activation(
                tcol.ap(8, 9), tcol.ap(7, 8), ACT.Sqrt), inc="sc", n=1)
            p.wait("vector", "sc", r2_tok)
            p.run("vector", lambda En, tt=tt: En.tensor_scalar(
                mr_col.ap(tt, tt + 1), tcol.ap(2, 3), r_col.ap(tt, tt + 1), -1.0,
                op0=AL.mult, op1=AL.mult))
            p.drain("vector")
            p.run("vector", lambda En, tt=tt: En.tensor_tensor(
                rr2_col.ap(tt, tt + 1), r_col.ap(tt, tt + 1), tcol.ap(8, 9), op=AL.mult))
            p.run("vector", lambda En, tt=tt: En.tensor_tensor(
                mrr2_col.ap(tt, tt + 1), mr_col.ap(tt, tt + 1), tcol.ap(8, 9), op=AL.mult))
            p.drain("vector")

            p.wait("vector", "t", stage_t)
            p.run("vector", lambda En, tt=tt, par=par: En.tensor_scalar(
                xn_stage.ap(), xld[par].ap(), r_col.ap(tt, tt + 1), mr_col.ap(tt, tt + 1),
                op0=AL.mult, op1=AL.add))
            p.run("vector", lambda En, tt=tt, par=par: En.tensor_scalar(
                normed_pre.ap(), xld[par].ap(), rr2_col.ap(tt, tt + 1), mrr2_col.ap(tt, tt + 1),
                op0=AL.mult, op1=AL.add))
            stg_tok = p.run("vector", lambda En: En.tensor_tensor(
                normed_stage.ap(), normed_pre.ap(), nw_bc.ap(), op=AL.mult), inc="v", n=1)
            ln_v_done[tt] = stg_tok

            p.wait("tensor", "v", stg_tok)
            for kind, stage, bank0 in (("xn", xn_stage, 0), ("nm", normed_stage, 2)):
                bank = bank0 + par
                if (kind, par) in evac_v:
                    p.wait("tensor", "v", evac_v[(kind, par)])
                pa_bf = PA[:, bank * 512:(bank + 1) * 512].bitcast(BF16)
                for c in range(NC):
                    inc = "t" if c == NC - 1 else None
                    ln_mm_t = p.run("tensor", lambda En, c=c, pa_bf=pa_bf, stage=stage: En.transpose(
                        pa_bf[:, c * 128:(c + 1) * 128], stage.ap(c * 128, (c + 1) * 128),
                        ident.ap()), inc=inc)
            stage_t = ln_mm_t

            p.wait("vector", "t", ln_mm_t)
            for kind, dstbuf, bank0 in (("xn", xn_fm, 0), ("nm", normed_fm, 2)):
                bank = bank0 + par
                pa_bf = PA[:, bank * 512:(bank + 1) * 512].bitcast(BF16)
                src3 = pa_bf.rearrange("p (c j) -> p c j", c=NC)
                dst3 = dstbuf.ap().rearrange("p (c t j) -> p c t j", c=NC, t=NT)[:, :, tt, :]
                evac_v[(kind, par)] = p.run("vector", lambda En, d=dst3, s=src3: En.tensor_copy(d, s),
                                            inc="v", n=1)
        ln_done_v = p.tok("v")

        wq_used_t = {}
        wq_tok = {}
        qk_evac_sc = {}
        last_t = 0

        def load_half(j, col0, buf):
            if j >= 2:
                p.wait("gpsimd", "t", wq_used_t.get(j - 2, 0))
            else:
                p.wait("gpsimd", "t", stage_t)
                p.wait("gpsimd", "v", ln_done_v)
                p.wait("gpsimd", "sc", p.tok("sc"))
            tokk = None
            for c in range(NC):
                tokk = p.run("gpsimd", lambda En, c=c, col0=col0, buf=buf: En.dma_start(
                    out=wq_buf[buf].ap(c * 512, (c + 1) * 512),
                    in_=wqkv_d[c * 128:(c + 1) * 128, col0:col0 + 512]), inc="gd", n=16)
            wq_tok[j] = tokk

        for j, (part, hf) in enumerate((("q", 0), ("q", 1), ("k", 0), ("k", 1))):
            col0 = (0 if part == "q" else D) + hf * 512
            buf = j % 2
            load_half(j, col0, buf)
            dstbuf = q_fm if part == "q" else k_fm
            bias_base = 0 if part == "q" else 8
            p.wait("tensor", "gd", wq_tok[j])
            p.wait("tensor", "v", ln_done_v)
            for ftl in range(4):
                chunk = hf * 4 + ftl
                for qb in range(4):
                    ppar = (ftl * 4 + qb) % 2
                    if ppar in qk_evac_sc:
                        p.wait("tensor", "sc", qk_evac_sc[ppar])
                    for c in range(NC):
                        inc = "t" if c == NC - 1 else None
                        last_t = p.run("tensor", lambda En, c=c, ftl=ftl, qb=qb, ppar=ppar, buf=buf: En.matmul(
                            PA[:, ppar * 512:(ppar + 1) * 512],
                            wq_buf[buf].ap(c * 512 + ftl * 128, c * 512 + ftl * 128 + 128),
                            xn_fm.ap(c * 2048 + qb * 512, c * 2048 + (qb + 1) * 512),
                            start=(c == 0), stop=(c == NC - 1)), inc=inc)
                    p.wait("scalar", "t", last_t)
                    bias = b_qkv_pp.ap(bias_base + chunk, bias_base + chunk + 1)
                    qk_evac_sc[ppar] = p.run("scalar", lambda En, ppar=ppar, chunk=chunk, qb=qb,
                                             dstbuf=dstbuf, bias=bias: En.activation(
                        dstbuf.ap(chunk * 2048 + qb * 512, chunk * 2048 + (qb + 1) * 512),
                        PA[:, ppar * 512:(ppar + 1) * 512], ACT.Identity, bias=bias, scale=1.0),
                        inc="sc", n=1)
            wq_used_t[j] = last_t

        slices = [(0, 512), (512, 1024), (1024, NR)]
        rope_prev_v = 0
        qk_sc_done = p.tok("sc")
        for srcbuf in (q_fm, k_fm):
            for c in range(NC):
                p.wait("tensor", "sc", qk_sc_done)
                p.wait("tensor", "v", max(rope_prev_v, consts_v))
                for si, (a, b) in enumerate(slices):
                    w = b - a
                    inc = "t" if si == 2 else None
                    rot_tok = p.run("tensor", lambda En, a=a, w=w, c=c, srcbuf=srcbuf: En.matmul(
                        PB[:, a:a + w], R2.ap(),
                        srcbuf.ap(c * 2048 + PRE + a, c * 2048 + PRE + a + w),
                        start=True, stop=True), inc=inc)
                p.wait("vector", "sc", qk_sc_done)
                p.wait("vector", "t", rot_tok)
                for si, (a, b) in enumerate(slices):
                    w = b - a
                    p.run("vector", lambda En, a=a, w=w, c=c, srcbuf=srcbuf: En.tensor_tensor(
                        rope_s1.ap(0, w), srcbuf.ap(c * 2048 + PRE + a, c * 2048 + PRE + a + w),
                        cos_sb.ap(a, a + w), op=AL.mult))
                    p.run("vector", lambda En, a=a, w=w: En.tensor_tensor(
                        rope_s2.ap(0, w), PB[:, a:a + w], sin_sb.ap(a, a + w), op=AL.mult))
                    inc = "v" if si == 2 else None
                    rope_prev_v = p.run("vector", lambda En, a=a, w=w, c=c, srcbuf=srcbuf: En.tensor_tensor(
                        srcbuf.ap(c * 2048 + PRE + a, c * 2048 + PRE + a + w),
                        rope_s1.ap(0, w), rope_s2.ap(0, w), op=AL.add), inc=inc)
        rope_done_v = rope_prev_v

        v_evac_v = {}
        for hf in range(2):
            j = 4 + hf
            load_half(j, 2 * D + hf * 512, j % 2)
            p.wait("tensor", "gd", wq_tok[j])
            p.wait("tensor", "v", rope_done_v)
            for tt in range(NT):
                par2 = (hf * NT + tt) % 2
                if par2 in v_evac_v:
                    p.wait("tensor", "v", v_evac_v[par2])
                for c in range(NC):
                    inc = "t" if c == NC - 1 else None
                    last_t = p.run("tensor", lambda En, c=c, tt=tt, par2=par2, buf=j % 2: En.matmul(
                        PB[:, par2 * 512:(par2 + 1) * 512],
                        xn_fm.ap(c * 2048 + tt * 128, c * 2048 + tt * 128 + 128),
                        wq_buf[buf].ap(c * 512, (c + 1) * 512),
                        start=(c == 0), stop=(c == NC - 1)), inc=inc)
                p.wait("vector", "t", last_t)
                dst3 = v_tm.ap().rearrange("p (t h j) -> p t h j", t=NT, h=H)[:, tt, hf * 8:(hf + 1) * 8, 0:64]
                src3 = PB[:, par2 * 512:(par2 + 1) * 512].rearrange("p (h j) -> p h j", h=8)
                bias3 = bias = b_v_bc.ap(hf * 512, (hf + 1) * 512).rearrange("p (h j) -> p h j", h=8)
                v_evac_v[par2] = p.run("vector", lambda En, d=dst3, s=src3, b3=bias3: En.tensor_tensor(
                    d, s, b3, op=AL.add), inc="v", n=1)
            wq_used_t[j] = last_t
        v_done_v = p.tok("v")

        router_v = 0
        p.wait("tensor", "gd", init_gd)
        for tt in range(NT):
            p.wait("tensor", "sc", qk_sc_done)
            p.wait("tensor", "v", router_v)
            for c in range(NC):
                inc = "t" if c == NC - 1 else None
                last_t = p.run("tensor", lambda En, c=c, tt=tt: En.matmul(
                    PA[:, 0:8],
                    normed_fm.ap(c * 2048 + tt * 128, c * 2048 + tt * 128 + 128),
                    rt_sb.ap(c * 8, (c + 1) * 8),
                    start=(c == 0), stop=(c == NC - 1)), inc=inc)
            p.wait("vector", "t", last_t)
            p.run("vector", lambda En: En.tensor_copy(lg_s.ap(), PA[:, 0:8]))
            p.drain("vector")
            p.run("vector", lambda En: En.max(mx8.ap(), lg_s.ap()))
            p.drain("vector")
            p.run("vector", lambda En: En.max_index(mi8.ap(), mx8.ap(), lg_s.ap()))
            p.drain("vector")
            p.run("vector", lambda En: En.tensor_copy(idxf.ap(0, 1), mi8.ap(0, 1)))
            p.run("vector", lambda En: En.tensor_copy(idxf.ap(1, 2), mi8.ap(1, 2)))
            d21_tok = p.runD("vector", lambda En: En.tensor_tensor(
                tcol.ap(9, 10), mx8.ap(1, 2), mx8.ap(0, 1), op=AL.subtract), inc="v", n=1)
            p.wait("scalar", "v", d21_tok)
            ee_tok = p.runD("scalar", lambda En: En.activation(
                tcol.ap(10, 11), tcol.ap(9, 10), ACT.Exp), inc="sc", n=1)
            p.wait("vector", "sc", ee_tok)
            p.run("vector", lambda En: En.tensor_scalar(m1m2.ap(0, 8), io8.ap(), idxf.ap(0, 1), None,
                                                        op0=AL.is_equal))
            p.run("vector", lambda En: En.tensor_scalar(m1m2.ap(8, 16), io8.ap(), idxf.ap(1, 2), None,
                                                        op0=AL.is_equal))
            p.run("vector", lambda En: En.tensor_scalar(tcol.ap(11, 12), tcol.ap(10, 11), 1.0, None,
                                                        op0=AL.add))
            p.drain("vector")
            p.run("vector", lambda En: En.reciprocal(w12.ap(0, 1), tcol.ap(11, 12)))
            p.drain("vector")
            p.run("vector", lambda En: En.tensor_scalar(w12.ap(1, 2), w12.ap(0, 1), -1.0, 1.0,
                                                        op0=AL.mult, op1=AL.add))
            p.drain("vector")
            p.run("vector", lambda En: En.tensor_scalar(m1m2.ap(0, 8), m1m2.ap(0, 8), w12.ap(0, 1), None,
                                                        op0=AL.mult))
            p.drain("vector")
            router_v = p.run("vector", lambda En, tt=tt: En.scalar_tensor_tensor(
                c_coef.ap(tt * 8, (tt + 1) * 8), m1m2.ap(8, 16), w12.ap(1, 2), m1m2.ap(0, 8),
                op0=AL.mult, op1=AL.add), inc="v", n=1)
        router_done_v = router_v

        pv_free_v = [0, 0]
        exp_sc = {0: 0, 1: 0}
        pv_t = {}
        pT_used_t = [0, 0, 0, 0]
        attn_last_v = 0
        first_st = True
        NG = 8

        for cp in range(H // 2):
            for qb in range(4):
                st_t = {}

                def do_st(slot, g):
                    nonlocal first_st
                    h = 2 * cp + slot
                    r0 = 64 * slot
                    p.wait("tensor", "sc", exp_sc[slot])
                    if first_st:
                        p.wait("tensor", "v", router_done_v)
                        first_st = False
                    tok = 0
                    for kk in range(2):
                        kt = 2 * g + kk
                        bank = 2 * slot + kk
                        inc = "t" if kk == 1 else None
                        tok = p.run("tensor", lambda En, kk=kk, kt=kt, bank=bank, cp=cp, r0=r0, qb=qb: En.matmul(
                            PA[:, bank * 512:(bank + 1) * 512],
                            k_fm.ap(cp * 2048 + kt * 128, cp * 2048 + kt * 128 + 128, r0, r0 + 64),
                            q_fm.ap(cp * 2048 + qb * 512, cp * 2048 + (qb + 1) * 512, r0, r0 + 64),
                            start=True, stop=True), inc=inc)
                    st_t[(slot, g)] = tok

                def do_exp(slot, g):
                    buf = 2 * slot + (g % 2)
                    p.wait("scalar", "t", st_t[(slot, g)])
                    p.wait("scalar", "t", pT_used_t[buf])
                    exp_sc[slot] = p.run("scalar", lambda En, slot=slot, buf=buf: En.activation(
                        pT[buf].ap(), PA[:, slot * 1024: slot * 1024 + 1024],
                        ACT.Exp, bias=negm.ap(), scale=0.125), inc="sc", n=1)
                    exp_sc[(slot, g)] = exp_sc[slot]

                def do_pv(slot, g):
                    h = 2 * cp + slot
                    p.wait("tensor", "sc", exp_sc[(slot, g)])
                    if g == 0:
                        p.wait("tensor", "v", pv_free_v[slot])
                    buf = 2 * slot + (g % 2)
                    tok = 0
                    for kk in range(2):
                        for qt in range(4):
                            st_ = (g == 0 and kk == 0 and qt == 0)
                            sp_ = (g == NG - 1 and kk == 1 and qt == 3)
                            inc = "t" if (kk == 1 and qt == 3) else None
                            tok = p.run("tensor", lambda En, g=g, kk=kk, qt=qt, h=h, slot=slot,
                                        buf=buf, st_=st_, sp_=sp_: En.matmul(
                                PB[:, slot * 512 + qt * 65: slot * 512 + qt * 65 + 65],
                                pT[buf].ap(kk * 512 + qt * 128, kk * 512 + qt * 128 + 128),
                                v_tm.ap((2 * g + kk) * 1040 + h * 65, (2 * g + kk) * 1040 + (h + 1) * 65),
                                start=st_, stop=sp_, skip_group_check=True), inc=inc)
                    pv_t[(slot, g)] = tok
                    pT_used_t[buf] = tok

                for g in range(NG):
                    do_st(0, g)
                    do_st(1, g)
                    p.drain("tensor")
                    do_exp(0, g)
                    do_exp(1, g)
                    if g > 0:
                        do_pv(0, g - 1)
                        do_pv(1, g - 1)
                    p.drain("tensor")
                do_pv(0, NG - 1)
                do_pv(1, NG - 1)
                p.drain("tensor")

                for slot in range(2):
                    h = 2 * cp + slot
                    p.wait("vector", "t", pv_t[(slot, NG - 1)])
                    for qt in range(4):
                        p.run("vector", lambda En, qt=qt, slot=slot: En.reciprocal(
                            rec4.ap(qt, qt + 1), PB[:, slot * 512 + qt * 65 + 64: slot * 512 + qt * 65 + 65]))
                    p.drain("vector")
                    for qt in range(4):
                        gslot = qb * 4 + qt
                        inc = "v" if qt == 3 else None
                        attn_last_v = p.run("vector", lambda En, qt=qt, slot=slot, gslot=gslot, h=h: En.tensor_scalar(
                            O_tm.ap(gslot * 1024 + h * 64, gslot * 1024 + (h + 1) * 64),
                            PB[:, slot * 512 + qt * 65: slot * 512 + qt * 65 + 64],
                            rec4.ap(qt, qt + 1), None, op0=AL.mult), inc=inc)
                    pv_free_v[slot] = attn_last_v
        attn_done_v = p.tok("v")
        last_exp_sc = max(exp_sc[0], exp_sc[1])

        p.wait("gpsimd", "t", wq_used_t[5])
        p.wait("gpsimd", "v", rope_done_v)
        for c in range(NC):
            wproj_tok = p.run("gpsimd", lambda En, c=c: En.dma_start(
                out=wproj_buf.ap(c * 1024, (c + 1) * 1024),
                in_=wproj_d[c * 128:(c + 1) * 128, :]), inc="gd", n=16)

        otr_evac_v = {}
        otr_v = [0] * NT
        p.wait("tensor", "v", attn_done_v)
        p.wait("tensor", "sc", last_exp_sc)
        for g in range(NT):
            par = g % 2
            if par in otr_evac_v:
                p.wait("tensor", "v", otr_evac_v[par])
            pa_bf = PA[:, par * 512:(par + 1) * 512].bitcast(BF16)
            for c in range(NC):
                inc = "t" if c == NC - 1 else None
                otr_tok = p.run("tensor", lambda En, c=c, g=g, pa_bf=pa_bf: En.transpose(
                    pa_bf[:, c * 128:(c + 1) * 128],
                    O_tm.ap(g * 1024 + c * 128, g * 1024 + (c + 1) * 128), ident.ap()), inc=inc)
            p.wait("vector", "t", otr_tok)
            src3 = pa_bf.rearrange("p (c j) -> p c j", c=NC)
            dst3 = O_fm.ap(0, NC * T).rearrange("p (c t j) -> p c t j", c=NC, t=NT)[:, :, g, :]
            otr_v[g] = p.run("vector", lambda En, d=dst3, s=src3: En.tensor_copy(d, s), inc="v", n=1)
            otr_evac_v[par] = otr_v[g]

        proj_evac_v = {}
        p.wait("tensor", "gd", wproj_tok)
        for tt in range(NT):
            par = tt % 2
            p.wait("tensor", "v", otr_v[tt])
            if par in proj_evac_v:
                p.wait("tensor", "v", proj_evac_v[par])
            for c in range(NC):
                for dh2 in range(2):
                    inc = "t" if (c == NC - 1 and dh2 == 1) else None
                    proj_tok = p.run("tensor", lambda En, c=c, dh2=dh2, tt=tt, par=par: En.matmul(
                        PB[:, par * 1024 + dh2 * 512: par * 1024 + (dh2 + 1) * 512],
                        O_fm.ap(c * 2048 + tt * 128, c * 2048 + tt * 128 + 128),
                        wproj_buf.ap(c * 1024 + dh2 * 512, c * 1024 + (dh2 + 1) * 512),
                        start=(c == 0), stop=(c == NC - 1)), inc=inc)
            p.wait("vector", "t", proj_tok)
            proj_evac_v[par] = p.run("vector", lambda En, tt=tt, par=par: En.tensor_tensor(
                s_tm.ap(tt * 1024, (tt + 1) * 1024), PB[:, par * 1024:(par + 1) * 1024],
                b_proj_bc.ap(), op=AL.add), inc="v", n=1)
        proj_t_done = p.tok("t")

        gup_slot_t = [0] * 4
        down_slot_t = [0] * 11
        hidden_used_t = [0] * 11
        psum_free_v = [0, 0]
        silu_used_v = [0, 0]
        moe_evac_v = {}
        hid = [(REG4 if i < 8 else REG1, (i if i < 8 else i - 8) * 2048) for i in range(11)]
        first_gpsimd_ffn = True
        first_moe = True
        gmm_by_i = [0] * 11

        for e in range(E):
            for hh in range(2):
                g_toks = [0] * 11
                u_toks = [0] * 11
                d_toks = [0] * 11
                def load_gup(il):
                    i = hh * 11 + il
                    sg_ = (2 * il) % 4
                    su_ = (2 * il + 1) % 4
                    p.wait("gpsimd", "t", gup_slot_t[sg_])
                    g_toks[il] = p.run("gpsimd", lambda En, e=e, i=i, sg_=sg_: En.dma_start(
                        out=gup_buf[sg_].ap(), in_=gup_d[e, i]), inc="gd", n=16)
                    p.wait("gpsimd", "t", gup_slot_t[su_])
                    u_toks[il] = p.run("gpsimd", lambda En, e=e, i=i, su_=su_: En.dma_start(
                        out=gup_buf[su_].ap(), in_=gup_d[e, HT + i]), inc="gd", n=16)

                if first_gpsimd_ffn:
                    p.wait("gpsimd", "t", proj_t_done)
                    p.wait("gpsimd", "sc", last_exp_sc)
                    first_gpsimd_ffn = False
                load_gup(0)
                load_gup(1)
                for il in range(11):
                    i = hh * 11 + il
                    p.wait("gpsimd", "t", down_slot_t[il])
                    d_toks[il] = p.run("gpsimd", lambda En, e=e, i=i, il=il: En.dma_start(
                        out=down_buf[il].ap(), in_=down_d[e, i]), inc="gd", n=16)

                for il in range(11):
                    i = hh * 11 + il
                    sg_ = (2 * il) % 4
                    su_ = (2 * il + 1) % 4
                    p.wait("tensor", "gd", u_toks[il])
                    for qb in range(4):
                        pp = (il * 4 + qb) % 2
                        p.wait("tensor", "v", psum_free_v[pp])
                        for c in range(NC):
                            inc = "t" if c == NC - 1 else None
                            gmm = p.run("tensor", lambda En, c=c, qb=qb, pp=pp, sg_=sg_: En.matmul(
                                PA[:, (2 * pp) * 512:(2 * pp) * 512 + 512],
                                gup_buf[sg_].ap(c * 128, (c + 1) * 128),
                                normed_fm.ap(c * 2048 + qb * 512, c * 2048 + (qb + 1) * 512),
                                start=(c == 0), stop=(c == NC - 1)), inc=inc)
                        for c in range(NC):
                            inc = "t" if c == NC - 1 else None
                            umm = p.run("tensor", lambda En, c=c, qb=qb, pp=pp, su_=su_: En.matmul(
                                PA[:, (2 * pp + 1) * 512:(2 * pp + 1) * 512 + 512],
                                gup_buf[su_].ap(c * 128, (c + 1) * 128),
                                normed_fm.ap(c * 2048 + qb * 512, c * 2048 + (qb + 1) * 512),
                                start=(c == 0), stop=(c == NC - 1)), inc=inc)
                        p.wait("scalar", "t", gmm)
                        p.wait("scalar", "v", silu_used_v[pp])
                        sil = p.run("scalar", lambda En, pp=pp: En.activation(
                            silu_st[pp].ap(), PA[:, (2 * pp) * 512:(2 * pp) * 512 + 512],
                            ACT.Silu), inc="sc", n=1)
                        p.wait("vector", "sc", sil)
                        p.wait("vector", "t", umm)
                        if qb == 0:
                            p.wait("vector", "t", hidden_used_t[il])
                        hbuf, hoff = hid[il]
                        hv = p.run("vector", lambda En, hbuf=hbuf, hoff=hoff, qb=qb, pp=pp: En.tensor_tensor(
                            hbuf.ap(hoff + qb * 512, hoff + (qb + 1) * 512),
                            silu_st[pp].ap(), PA[:, (2 * pp + 1) * 512:(2 * pp + 1) * 512 + 512],
                            op=AL.mult), inc="v", n=1)
                        psum_free_v[pp] = hv
                        silu_used_v[pp] = hv
                    gmm_by_i[il] = umm
                    gup_slot_t[sg_] = umm
                    gup_slot_t[su_] = umm
                    if il + 2 <= 10:
                        load_gup(il + 2)
                hid_all_v = p.tok("v")

                dmm = 0
                for tt in range(NT):
                    par2 = tt % 2
                    p.wait("tensor", "v", hid_all_v)
                    p.wait("tensor", "gd", d_toks[10])
                    if par2 in moe_evac_v:
                        p.wait("tensor", "v", moe_evac_v[par2])
                    for il in range(11):
                        hbuf, hoff = hid[il]
                        for dh2 in range(2):
                            st_ = (il == 0)
                            sp_ = (il == 10)
                            inc = "t" if (il == 10 and dh2 == 1) else None
                            dmm = p.run("tensor", lambda En, il=il, dh2=dh2, tt=tt, par2=par2,
                                        st_=st_, sp_=sp_, hbuf=hbuf, hoff=hoff: En.matmul(
                                PB[:, par2 * 1024 + dh2 * 512: par2 * 1024 + (dh2 + 1) * 512],
                                hbuf.ap(hoff + tt * 128, hoff + tt * 128 + 128),
                                down_buf[il].ap(dh2 * 512, (dh2 + 1) * 512),
                                start=st_, stop=sp_), inc=inc)
                    p.wait("vector", "t", dmm)
                    ccol = c_coef.ap(tt * 8 + e, tt * 8 + e + 1)
                    if first_moe:
                        moe_evac_v[par2] = p.run("vector", lambda En, tt=tt, par2=par2, ccol=ccol: En.tensor_scalar(
                            moe.ap(tt * 1024, (tt + 1) * 1024),
                            PB[:, par2 * 1024:(par2 + 1) * 1024], ccol, None, op0=AL.mult),
                            inc="v", n=1)
                    else:
                        moe_evac_v[par2] = p.run("vector", lambda En, tt=tt, par2=par2, ccol=ccol: En.scalar_tensor_tensor(
                            moe.ap(tt * 1024, (tt + 1) * 1024),
                            PB[:, par2 * 1024:(par2 + 1) * 1024], ccol,
                            moe.ap(tt * 1024, (tt + 1) * 1024),
                            op0=AL.mult, op1=AL.add), inc="v", n=1)
                if first_moe:
                    first_moe = False
                for il in range(11):
                    hidden_used_t[il] = dmm
                    down_slot_t[il] = dmm
        ffn_t_done = p.tok("t")
        ffn_sc_done = p.tok("sc")

        fin_v = [0] * NT
        store_sd = [0] * NT
        p.wait("sync", "t", ffn_t_done)
        p.wait("sync", "sc", ffn_sc_done)
        for tt in range(NT):
            par = tt % 2
            if tt >= 2:
                p.wait("sync", "v", fin_v[tt - 2])
            x_tok = p.run("sync", lambda En, tt=tt, par=par: En.dma_start(
                out=fx[par].ap(), in_=x_d[tt * 128:(tt + 1) * 128, :]), inc="sd", n=16)
            p.wait("vector", "sd", x_tok)
            p.run("vector", lambda En, tt=tt, par=par: En.tensor_scalar(
                fscr1.ap(), fx[par].ap(), r_col.ap(tt, tt + 1), mr_col.ap(tt, tt + 1),
                op0=AL.mult, op1=AL.add))
            ffb = p.run("vector", lambda En, tt=tt: En.scalar_tensor_tensor(
                fscr2.ap(), moe.ap(tt * 1024, (tt + 1) * 1024), alpha_s.ap(), fscr1.ap(),
                op0=AL.mult, op1=AL.add), inc="v", n=1)
            p.wait("scalar", "v", ffb)
            g_tok = p.run("scalar", lambda En: En.activation(
                fscr1.ap(), fscr2.ap(), ACT.Silu), inc="sc", n=1)
            p.wait("vector", "sc", g_tok)
            p.run("vector", lambda En, tt=tt: En.tensor_tensor(
                fscr2.ap(), s_tm.ap(tt * 1024, (tt + 1) * 1024), fscr1.ap(), op=AL.mult))
            if tt >= 2:
                p.wait("vector", "sd", store_sd[tt - 2])
            fin_v[tt] = p.run("vector", lambda En, par=par: En.tensor_tensor(
                fout[par].ap(), fx[par].ap(), fscr2.ap(), op=AL.add), inc="v", n=1)
            p.wait("sync", "v", fin_v[tt])
            store_sd[tt] = p.run("sync", lambda En, tt=tt, par=par: En.dma_start(
                out=y_d[tt * 128:(tt + 1) * 128, :], in_=fout[par].ap()), inc="sd", n=16)
        p.wait("sync", "sd", p.tok("sd"))

        @block.tensor
        def _(En):
            for f in p.prog["tensor"]:
                f(En)

        @block.vector
        def _(En):
            for f in p.prog["vector"]:
                f(En)

        @block.scalar
        def _(En):
            for f in p.prog["scalar"]:
                f(En)

        @block.sync
        def _(En):
            for f in p.prog["sync"]:
                f(En)

        @block.gpsimd
        def _(En):
            for f in p.prog["gpsimd"]:
                f(En)

    return nc


_NC_CACHE = None


def kernel(**inputs):
    global _NC_CACHE
    x = np.ascontiguousarray(np.asarray(inputs["L0_attn_x"], dtype=np.float32))
    sin = np.asarray(inputs["L0_attn_sin"], dtype=np.float32)
    cos = np.asarray(inputs["L0_attn_cos"], dtype=np.float32)
    w_qkv = np.ascontiguousarray(np.asarray(inputs["L0_attn_w_qkv"], dtype=np.float32))
    b_qkv = np.asarray(inputs["L0_attn_b_qkv"], dtype=np.float32)
    w_proj = np.ascontiguousarray(np.asarray(inputs["L0_attn_w_proj"], dtype=np.float32))
    b_proj = np.asarray(inputs["L0_attn_b_proj"], dtype=np.float32)
    nw = np.asarray(inputs["L0_ffn_norm_weight"], dtype=np.float32)
    rw = np.asarray(inputs["L0_ffn_router_weight"], dtype=np.float32)
    gup_w = np.asarray(inputs["L0_ffn_gate_up_weight"], dtype=np.float32)
    down_w = np.asarray(inputs["L0_ffn_down_weight"], dtype=np.float32)
    alpha = np.asarray(inputs["L0_ffn_residual_alpha"], dtype=np.float32)

    sin_t = np.ascontiguousarray(np.tile(sin.T, (2, 1)))
    cos_t = np.ascontiguousarray(np.tile(cos.T, (2, 1)))
    b_qkv_pp = np.ascontiguousarray(b_qkv.reshape(24, 128).T)
    b_v_bc = np.ascontiguousarray(np.tile(b_qkv[2 * D:3 * D][None, :], (128, 1)))
    b_proj_bc = np.ascontiguousarray(np.tile(b_proj[None, :], (128, 1)))
    nw_bc = np.ascontiguousarray(np.tile(nw[None, :], (128, 1)))
    rt4 = np.ascontiguousarray(rw.T.reshape(NC, 128, E))
    gup = np.ascontiguousarray(
        gup_w.reshape(E, 2 * HT, 128, NC, 128).transpose(0, 1, 4, 3, 2).reshape(E, 2 * HT, 128, D))
    down = np.ascontiguousarray(
        down_w.transpose(0, 2, 1).reshape(E, HT, 128, D))
    alpha_t = np.ascontiguousarray(np.tile(alpha.reshape(1, 1), (128, 1)))

    if _NC_CACHE is None:
        _NC_CACHE = build_nc()
    nc = _NC_CACHE

    shared = dict(sin_t=sin_t, cos_t=cos_t, w_qkv=w_qkv, b_qkv_pp=b_qkv_pp,
                  b_v_bc=b_v_bc, w_proj=w_proj, b_proj_bc=b_proj_bc, nw_bc=nw_bc,
                  rt4=rt4, gup=gup, down=down, alpha_t=alpha_t)
    in_maps = [dict(x=np.ascontiguousarray(x[b]), **shared) for b in range(8)]
    res = run_bass_kernel_spmd(nc, in_maps, core_ids=list(range(8)))
    return np.stack([r["y"] for r in res.results], axis=0)


# version 06
# speedup vs baseline: 1.3861x; 1.3861x over previous
import numpy as np
import concourse.bass as bass
import concourse.mybir as mybir
from concourse.alu_op_type import AluOpType
from concourse.bass_utils import run_bass_kernel_spmd
from contextlib import ExitStack

F32 = mybir.dt.float32
BF16 = mybir.dt.bfloat16
U32 = mybir.dt.uint32
ACT = mybir.ActivationFunctionType
AL = AluOpType

T = 2048
D = 1024
H = 16
DH = 64
HID = 2816
E = 8
NR = 1369
PRE = T - NR
NT = T // 128
NC = D // 128
HT = HID // 128
M_SOFT = 32.0
CAP = 768
CT = CAP // 128

ENGINES = ("tensor", "vector", "scalar", "sync", "gpsimd")


class Plan:

    def __init__(self, nc, sem_names):
        self.nc = nc
        self.prog = {e: [] for e in ENGINES}
        self.count = {s: 0 for s in sem_names}
        self.handles = {}
        self.waited = {e: {s: 0 for s in sem_names} for e in ENGINES}

    def wait(self, eng, sem, val=None):
        v = self.count[sem] if val is None else val
        if v is None or v <= 0 or v <= self.waited[eng][sem]:
            return
        self.waited[eng][sem] = v
        h = self.handles[sem]
        self.prog[eng].append(lambda En, h=h, v=v: En.wait_ge(h, v))

    def run(self, eng, fn, inc=None, n=1):
        if inc is None:
            self.prog[eng].append(fn)
            return self.count.get("_", 0)
        h = self.handles[inc]
        self.count[inc] += n
        self.prog[eng].append(lambda En, fn=fn, h=h, n=n: fn(En).then_inc(h, n))
        return self.count[inc]

    def tok(self, sem):
        return self.count[sem]

    def drain(self, eng):
        self.prog[eng].append(lambda En: En.drain())

    def runD(self, eng, fn, inc, n=1):
        self.prog[eng].append(fn)
        h = self.handles[inc]
        self.count[inc] += n
        self.prog[eng].append(lambda En, h=h, n=n: En.drain().then_inc(h, n))
        return self.count[inc]


class Buf:

    def __init__(self, arena, off_units, width_elems, dtype):
        self.arena = arena
        self.off = off_units
        self.width = width_elems
        self.dtype = dtype
        self.esz = 1 if dtype == BF16 else 2

    def ap(self, c0=0, c1=None, r0=0, r1=128):
        if c1 is None:
            c1 = self.width
        a = self.arena[r0:r1, self.off + c0 * self.esz: self.off + c1 * self.esz]
        if self.dtype != BF16:
            a = a.bitcast(self.dtype)
        return a


def build_nc():
    nc = bass.Bass("TRN2", target_bir_lowering=False)
    x_d = nc.dram_tensor("x", [T, D], F32, kind="ExternalInput").ap()
    sin_d = nc.dram_tensor("sin_t", [128, NR], F32, kind="ExternalInput").ap()
    cos_d = nc.dram_tensor("cos_t", [128, NR], F32, kind="ExternalInput").ap()
    wqkv_d = nc.dram_tensor("w_qkv", [D, 3 * D], F32, kind="ExternalInput").ap()
    bqkv_d = nc.dram_tensor("b_qkv_pp", [128, 24], F32, kind="ExternalInput").ap()
    bv_d = nc.dram_tensor("b_v_bc", [128, D], F32, kind="ExternalInput").ap()
    wproj_d = nc.dram_tensor("w_proj", [D, D], F32, kind="ExternalInput").ap()
    bproj_d = nc.dram_tensor("b_proj_bc", [128, D], F32, kind="ExternalInput").ap()
    nw_d = nc.dram_tensor("nw_bc", [128, D], F32, kind="ExternalInput").ap()
    rt_d = nc.dram_tensor("rt4", [NC, 128, E], F32, kind="ExternalInput").ap()
    gup_d = nc.dram_tensor("gup", [E, 2 * HT, 128, D], F32, kind="ExternalInput").ap()
    down_d = nc.dram_tensor("down", [E, HT, 128, D], F32, kind="ExternalInput").ap()
    alpha_d = nc.dram_tensor("alpha_t", [128, 1], F32, kind="ExternalInput").ap()
    y_d = nc.dram_tensor("y", [T, D], F32, kind="ExternalOutput").ap()
    nrm_d = nc.dram_tensor("nrm_sc", [T, D], BF16, kind="Internal").ap()
    tokl_d = nc.dram_tensor("tok_l", [E * CAP + 256], U32, kind="Internal").ap()
    dstl_d = nc.dram_tensor("dst_l", [E * CAP + 256], U32, kind="Internal").ap()
    moeab_d = nc.dram_tensor("moe_ab", [2 * T + E * CAP, D], BF16, kind="Internal").ap()

    with ExitStack() as ctx:
        en = ctx.enter_context
        arena_h = en(nc.sbuf_tensor("arena", [128, 106240], BF16))
        arena = arena_h[:]
        PA = en(nc.psum_tensor("PA", [128, 2048], F32))
        PB = en(nc.psum_tensor("PB", [128, 2048], F32))

        sems = ["t", "v", "sc", "gd", "sd"]
        p = Plan(nc, sems)
        for s in sems:
            p.handles[s] = en(nc.semaphore(s))
        block = en(nc.Block())

        off = [0]

        def mk(width_elems, dtype, align=16):
            units = width_elems * (1 if dtype == BF16 else 2)
            off[0] = (off[0] + align - 1) // align * align
            b = Buf(arena, off[0], width_elems, dtype)
            off[0] += units
            return b

        ident = mk(128, BF16)
        R2 = mk(128, BF16)
        io8 = mk(E, F32)
        alpha_s = mk(1, F32)
        b_qkv_pp = mk(24, F32)
        r_col = mk(NT, F32)
        mr_col = mk(NT, F32)
        rr2_col = mk(NT, F32)
        mrr2_col = mk(NT, F32)
        tcol = mk(16, F32)
        lg_s = mk(E, F32)
        mx8 = mk(E, F32)
        mi8 = mk(E, U32)
        idxf = mk(2, F32)
        w12 = mk(2, F32)
        m1m2 = mk(2 * E, F32)
        c_coef = mk(128, F32)
        pidx = mk(4, F32)
        rec4 = mk(4, F32)
        negm = mk(1, F32)
        wA_col = mk(NT, F32)
        wB_col = mk(NT, F32)
        b_v_bc = mk(D, F32)
        b_proj_bc = mk(D, F32)
        nw_bc = mk(D, F32)
        rt_sb = mk(NC * E, BF16)

        REG1 = mk(16384, BF16, 256)
        REG2 = mk(16384, BF16, 256)
        REG3 = mk(16384, BF16, 256)
        REG4 = mk(16640, BF16, 256)
        REG5 = mk(16384, BF16, 256)
        REG6 = mk(16384, BF16, 256)
        assert off[0] <= 106240, off[0]

        xn_fm, O_tm = REG1, REG1
        q_fm, s_tm = REG2, REG2
        k_fm, moe = REG3, REG3
        v_tm, O_fm = REG4, REG4
        normed_fm = REG5

        def r6(off_u, width_elems, dtype):
            return Buf(arena, REG6.off + off_u, width_elems, dtype)

        xld = [r6(0, D, F32), r6(2048, D, F32)]
        sqb = r6(4096, D, F32)
        xn_stage = r6(6144, D, BF16)
        normed_pre = r6(7168, D, F32)
        normed_stage = r6(9216, D, BF16)
        sin_sb = r6(10256, NR, F32)
        cos_sb = r6(13024, NR, F32)
        wq_buf = [r6(0, 4096, BF16), r6(4096, 4096, BF16)]
        rope_s1 = r6(8192, 512, F32)
        rope_s2 = r6(9232, 512, F32)
        pT = [r6(0, 1024, BF16), r6(1024, 1024, BF16), r6(2048, 1024, BF16), r6(3072, 1024, BF16)]
        wproj_buf = r6(4096, NC * D, BF16)
        gup_buf = [r6(i * 1024, D, BF16) for i in range(4)]
        down_buf = [r6(4096 + i * 1024, D, BF16) for i in range(11)]
        silu_st = [r6(15360, 512, BF16), r6(15872, 512, BF16)]
        colidx = r6(8192, 128, F32)
        scrA = r6(8448, 128, F32)
        idx_cols = r6(13056, 32, F32)
        fx = [r6(0, D, F32), r6(2048, D, F32)]
        fscr1 = r6(4096, D, F32)
        fscr2 = r6(6144, D, F32)
        fout = [r6(8192, D, F32), r6(10240, D, F32)]

        for dst, src in [(b_qkv_pp.ap(), bqkv_d), (nw_bc.ap(), nw_d),
                         (b_v_bc.ap(), bv_d), (b_proj_bc.ap(), bproj_d),
                         (alpha_s.ap(), alpha_d), (sin_sb.ap(), sin_d),
                         (cos_sb.ap(), cos_d)]:
            init_sd = p.run("sync", lambda En, d=dst, s=src: En.dma_start(out=d, in_=s), inc="sd", n=16)

        p.run("gpsimd", lambda En: En.iota(io8.ap(), [[1, E]], channel_multiplier=0,
                                           allow_small_or_imprecise_dtypes=True), inc="gd", n=1)
        p.run("gpsimd", lambda En: En.iota(colidx.ap(), [[1, 128]], channel_multiplier=0,
                                           allow_small_or_imprecise_dtypes=True), inc="gd", n=1)
        p.run("gpsimd", lambda En: En.iota(pidx.ap(0, 1), [[0, 1]], channel_multiplier=1,
                                           allow_small_or_imprecise_dtypes=True), inc="gd", n=1)
        vone = v_tm.ap().rearrange("p (t h j) -> p t h j", t=NT, h=H)[:, :, :, 64:65]
        p.run("gpsimd", lambda En: En.memset(vone, 1.0), inc="gd", n=1)
        p.run("gpsimd", lambda En: En.iota(
            scrA.ap().rearrange("p (z y x) -> p z y x", z=2, y=2), [[0, 2], [2, 2], [0, 32]],
            base=-1, channel_multiplier=0, allow_small_or_imprecise_dtypes=True), inc="gd", n=1)
        init_gd = p.run("gpsimd", lambda En: En.dma_start(
            out=rt_sb.ap().rearrange("p (c e) -> p c e", c=NC),
            in_=rt_d.rearrange("c p e -> p c e")), inc="gd", n=16)

        p.wait("sync", "sd", init_sd)
        p.wait("gpsimd", "gd", init_gd)
        p.wait("vector", "gd", init_gd)
        p.wait("vector", "sd", init_sd)
        p.run("vector", lambda En: En.tensor_scalar(ident.ap(), colidx.ap(), pidx.ap(0, 1), None,
                                                    op0=AL.is_equal))
        p.run("vector", lambda En: En.tensor_scalar(pidx.ap(1, 2), pidx.ap(0, 1), 32.0, None, op0=AL.add))
        p.run("vector", lambda En: En.tensor_scalar(pidx.ap(2, 3), pidx.ap(0, 1), -32.0, None, op0=AL.add))
        p.run("vector", lambda En: En.tensor_scalar(tcol.ap(0, 1), pidx.ap(0, 1), 64.0, None, op0=AL.is_ge))
        p.drain("vector")
        p.run("vector", lambda En: En.tensor_scalar(tcol.ap(1, 2), tcol.ap(0, 1), -64.0, None, op0=AL.mult))
        p.drain("vector")
        p.run("vector", lambda En: En.tensor_tensor(tcol.ap(1, 2), pidx.ap(0, 1), tcol.ap(1, 2), op=AL.add))
        p.drain("vector")
        p.run("vector", lambda En: En.tensor_scalar(tcol.ap(2, 3), tcol.ap(1, 2), 32.0, None, op0=AL.is_lt))
        p.drain("vector")
        p.run("vector", lambda En: En.tensor_scalar(tcol.ap(3, 4), tcol.ap(2, 3), -1.0, 1.0,
                                                    op0=AL.mult, op1=AL.add))
        p.drain("vector")
        p.run("vector", lambda En: En.tensor_scalar(c_coef.ap(), colidx.ap(), pidx.ap(1, 2), None,
                                                    op0=AL.is_equal))
        p.run("vector", lambda En: En.tensor_scalar(colidx.ap(), colidx.ap(), pidx.ap(2, 3), None,
                                                    op0=AL.is_equal))
        p.drain("vector")
        p.run("vector", lambda En: En.tensor_scalar(c_coef.ap(), c_coef.ap(), tcol.ap(2, 3), None,
                                                    op0=AL.mult))
        p.run("vector", lambda En: En.tensor_scalar(colidx.ap(), colidx.ap(), tcol.ap(3, 4), None,
                                                    op0=AL.mult))
        p.drain("vector")
        p.run("vector", lambda En: En.tensor_tensor(c_coef.ap(), c_coef.ap(), colidx.ap(), op=AL.add))
        p.drain("vector")
        p.run("vector", lambda En: En.tensor_tensor(R2.ap(), c_coef.ap(), scrA.ap(), op=AL.mult))
        p.run("vector", lambda En: En.memset(negm.ap(), -M_SOFT))
        p.run("vector", lambda En: En.tensor_scalar_min(alpha_s.ap(), alpha_s.ap(), 2.0))
        consts_v = p.run("vector", lambda En: En.tensor_scalar_max(alpha_s.ap(), alpha_s.ap(), -0.5),
                         inc="v", n=1)

        ln_v_done = [0] * NT
        ln_sc_done = [0] * NT
        evac_v = {}
        stage_t = 0
        sq_v = [0] * NT
        ln_mm_t = 0
        nrm_st_tok = 0

        for tt in range(NT):
            par = tt % 2
            if tt >= 2:
                p.wait("sync", "v", ln_v_done[tt - 2])
                p.wait("sync", "sc", ln_sc_done[tt - 2])
            x_tok = p.run("sync", lambda En, tt=tt, par=par: En.dma_start(
                out=xld[par].ap(), in_=x_d[tt * 128:(tt + 1) * 128, :]), inc="sd", n=16)

            p.wait("scalar", "sd", x_tok)
            if tt >= 1:
                p.wait("scalar", "v", sq_v[tt - 1])
            ln_sc_done[tt] = p.run("scalar", lambda En, par=par: En.activation(
                sqb.ap(), xld[par].ap(), ACT.Square), inc="sc", n=1)

            p.wait("vector", "sd", x_tok)
            p.wait("vector", "sc", ln_sc_done[tt])
            p.run("vector", lambda En, par=par: En.reduce_sum(tcol.ap(0, 1), xld[par].ap(),
                                                              axis=mybir.AxisListType.X))
            sq_v[tt] = p.run("vector", lambda En: En.reduce_sum(tcol.ap(1, 2), sqb.ap(),
                                                                axis=mybir.AxisListType.X), inc="v", n=1)
            p.drain("vector")
            p.run("vector", lambda En: En.tensor_scalar(tcol.ap(2, 3), tcol.ap(0, 1), 1.0 / D, None,
                                                        op0=AL.mult))
            p.run("vector", lambda En: En.tensor_scalar(tcol.ap(3, 4), tcol.ap(1, 2), 1.0 / D, None,
                                                        op0=AL.mult))
            p.drain("vector")
            p.run("vector", lambda En: En.tensor_tensor(tcol.ap(4, 5), tcol.ap(2, 3), tcol.ap(2, 3),
                                                        op=AL.mult))
            p.drain("vector")
            p.run("vector", lambda En: En.tensor_tensor(tcol.ap(3, 4), tcol.ap(3, 4), tcol.ap(4, 5),
                                                        op=AL.subtract))
            p.drain("vector")
            p.run("vector", lambda En: En.tensor_scalar(tcol.ap(4, 5), tcol.ap(3, 4), 1e-5, None,
                                                        op0=AL.add))
            p.drain("vector")
            p.run("vector", lambda En: En.reciprocal(tcol.ap(5, 6), tcol.ap(4, 5)))
            p.drain("vector")
            p.run("vector", lambda En: En.tensor_tensor(tcol.ap(6, 7), tcol.ap(3, 4), tcol.ap(5, 6),
                                                        op=AL.mult))
            p.drain("vector")
            p.run("vector", lambda En: En.tensor_scalar(tcol.ap(6, 7), tcol.ap(6, 7), 1e-6, None,
                                                        op0=AL.add))
            p.drain("vector")
            rv2_tok = p.runD("vector", lambda En: En.reciprocal(tcol.ap(7, 8), tcol.ap(6, 7)),
                             inc="v", n=1)
            p.wait("scalar", "v", rv2_tok)
            p.run("scalar", lambda En, tt=tt: En.activation(
                r_col.ap(tt, tt + 1), tcol.ap(5, 6), ACT.Sqrt))
            p.drain("scalar")
            r2_tok = p.runD("scalar", lambda En: En.activation(
                tcol.ap(8, 9), tcol.ap(7, 8), ACT.Sqrt), inc="sc", n=1)
            p.wait("vector", "sc", r2_tok)
            p.run("vector", lambda En, tt=tt: En.tensor_scalar(
                mr_col.ap(tt, tt + 1), tcol.ap(2, 3), r_col.ap(tt, tt + 1), -1.0,
                op0=AL.mult, op1=AL.mult))
            p.drain("vector")
            p.run("vector", lambda En, tt=tt: En.tensor_tensor(
                rr2_col.ap(tt, tt + 1), r_col.ap(tt, tt + 1), tcol.ap(8, 9), op=AL.mult))
            p.run("vector", lambda En, tt=tt: En.tensor_tensor(
                mrr2_col.ap(tt, tt + 1), mr_col.ap(tt, tt + 1), tcol.ap(8, 9), op=AL.mult))
            p.drain("vector")

            p.wait("vector", "t", stage_t)
            if tt >= 1:
                p.wait("vector", "sd", nrm_st_tok)
            p.run("vector", lambda En, tt=tt, par=par: En.tensor_scalar(
                xn_stage.ap(), xld[par].ap(), r_col.ap(tt, tt + 1), mr_col.ap(tt, tt + 1),
                op0=AL.mult, op1=AL.add))
            p.run("vector", lambda En, tt=tt, par=par: En.tensor_scalar(
                normed_pre.ap(), xld[par].ap(), rr2_col.ap(tt, tt + 1), mrr2_col.ap(tt, tt + 1),
                op0=AL.mult, op1=AL.add))
            stg_tok = p.run("vector", lambda En: En.tensor_tensor(
                normed_stage.ap(), normed_pre.ap(), nw_bc.ap(), op=AL.mult), inc="v", n=1)
            ln_v_done[tt] = stg_tok
            p.wait("sync", "v", stg_tok)
            nrm_st_tok = p.run("sync", lambda En, tt=tt: En.dma_start(
                out=nrm_d[tt * 128:(tt + 1) * 128, :], in_=normed_stage.ap()), inc="sd", n=16)

            p.wait("tensor", "v", stg_tok)
            for kind, stage, bank0 in (("xn", xn_stage, 0), ("nm", normed_stage, 2)):
                bank = bank0 + par
                if (kind, par) in evac_v:
                    p.wait("tensor", "v", evac_v[(kind, par)])
                pa_bf = PA[:, bank * 512:(bank + 1) * 512].bitcast(BF16)
                for c in range(NC):
                    inc = "t" if c == NC - 1 else None
                    ln_mm_t = p.run("tensor", lambda En, c=c, pa_bf=pa_bf, stage=stage: En.transpose(
                        pa_bf[:, c * 128:(c + 1) * 128], stage.ap(c * 128, (c + 1) * 128),
                        ident.ap()), inc=inc)
            stage_t = ln_mm_t

            p.wait("vector", "t", ln_mm_t)
            for kind, dstbuf, bank0 in (("xn", xn_fm, 0), ("nm", normed_fm, 2)):
                bank = bank0 + par
                pa_bf = PA[:, bank * 512:(bank + 1) * 512].bitcast(BF16)
                src3 = pa_bf.rearrange("p (c j) -> p c j", c=NC)
                dst3 = dstbuf.ap().rearrange("p (c t j) -> p c t j", c=NC, t=NT)[:, :, tt, :]
                evac_v[(kind, par)] = p.run("vector", lambda En, d=dst3, s=src3: En.tensor_copy(d, s),
                                            inc="v", n=1)
        ln_done_v = p.tok("v")

        wq_used_t = {}
        wq_tok = {}
        qk_evac_sc = {}
        last_t = 0

        def load_half(j, col0, buf):
            if j >= 2:
                p.wait("gpsimd", "t", wq_used_t.get(j - 2, 0))
            else:
                p.wait("gpsimd", "t", stage_t)
                p.wait("gpsimd", "v", ln_done_v)
                p.wait("gpsimd", "sc", p.tok("sc"))
            tokk = None
            for c in range(NC):
                tokk = p.run("gpsimd", lambda En, c=c, col0=col0, buf=buf: En.dma_start(
                    out=wq_buf[buf].ap(c * 512, (c + 1) * 512),
                    in_=wqkv_d[c * 128:(c + 1) * 128, col0:col0 + 512]), inc="gd", n=16)
            wq_tok[j] = tokk

        for j, (part, hf) in enumerate((("q", 0), ("q", 1), ("k", 0), ("k", 1))):
            col0 = (0 if part == "q" else D) + hf * 512
            buf = j % 2
            load_half(j, col0, buf)
            dstbuf = q_fm if part == "q" else k_fm
            bias_base = 0 if part == "q" else 8
            p.wait("tensor", "gd", wq_tok[j])
            p.wait("tensor", "v", ln_done_v)
            for ftl in range(4):
                chunk = hf * 4 + ftl
                for qb in range(4):
                    ppar = (ftl * 4 + qb) % 2
                    if ppar in qk_evac_sc:
                        p.wait("tensor", "sc", qk_evac_sc[ppar])
                    for c in range(NC):
                        inc = "t" if c == NC - 1 else None
                        last_t = p.run("tensor", lambda En, c=c, ftl=ftl, qb=qb, ppar=ppar, buf=buf: En.matmul(
                            PA[:, ppar * 512:(ppar + 1) * 512],
                            wq_buf[buf].ap(c * 512 + ftl * 128, c * 512 + ftl * 128 + 128),
                            xn_fm.ap(c * 2048 + qb * 512, c * 2048 + (qb + 1) * 512),
                            start=(c == 0), stop=(c == NC - 1)), inc=inc)
                    p.wait("scalar", "t", last_t)
                    bias = b_qkv_pp.ap(bias_base + chunk, bias_base + chunk + 1)
                    qk_evac_sc[ppar] = p.run("scalar", lambda En, ppar=ppar, chunk=chunk, qb=qb,
                                             dstbuf=dstbuf, bias=bias: En.activation(
                        dstbuf.ap(chunk * 2048 + qb * 512, chunk * 2048 + (qb + 1) * 512),
                        PA[:, ppar * 512:(ppar + 1) * 512], ACT.Identity, bias=bias, scale=1.0),
                        inc="sc", n=1)
            wq_used_t[j] = last_t

        slices = [(0, 512), (512, 1024), (1024, NR)]
        rope_prev_v = 0
        qk_sc_done = p.tok("sc")
        for srcbuf in (q_fm, k_fm):
            for c in range(NC):
                p.wait("tensor", "sc", qk_sc_done)
                p.wait("tensor", "v", max(rope_prev_v, consts_v))
                for si, (a, b) in enumerate(slices):
                    w = b - a
                    inc = "t" if si == 2 else None
                    rot_tok = p.run("tensor", lambda En, a=a, w=w, c=c, srcbuf=srcbuf: En.matmul(
                        PB[:, a:a + w], R2.ap(),
                        srcbuf.ap(c * 2048 + PRE + a, c * 2048 + PRE + a + w),
                        start=True, stop=True), inc=inc)
                p.wait("vector", "sc", qk_sc_done)
                p.wait("vector", "t", rot_tok)
                for si, (a, b) in enumerate(slices):
                    w = b - a
                    p.run("vector", lambda En, a=a, w=w, c=c, srcbuf=srcbuf: En.tensor_tensor(
                        rope_s1.ap(0, w), srcbuf.ap(c * 2048 + PRE + a, c * 2048 + PRE + a + w),
                        cos_sb.ap(a, a + w), op=AL.mult))
                    p.run("vector", lambda En, a=a, w=w: En.tensor_tensor(
                        rope_s2.ap(0, w), PB[:, a:a + w], sin_sb.ap(a, a + w), op=AL.mult))
                    inc = "v" if si == 2 else None
                    rope_prev_v = p.run("vector", lambda En, a=a, w=w, c=c, srcbuf=srcbuf: En.tensor_tensor(
                        srcbuf.ap(c * 2048 + PRE + a, c * 2048 + PRE + a + w),
                        rope_s1.ap(0, w), rope_s2.ap(0, w), op=AL.add), inc=inc)
        rope_done_v = rope_prev_v

        v_evac_v = {}
        for hf in range(2):
            j = 4 + hf
            load_half(j, 2 * D + hf * 512, j % 2)
            p.wait("tensor", "gd", wq_tok[j])
            p.wait("tensor", "v", rope_done_v)
            for tt in range(NT):
                par2 = (hf * NT + tt) % 2
                if par2 in v_evac_v:
                    p.wait("tensor", "v", v_evac_v[par2])
                for c in range(NC):
                    inc = "t" if c == NC - 1 else None
                    last_t = p.run("tensor", lambda En, c=c, tt=tt, par2=par2, buf=j % 2: En.matmul(
                        PB[:, par2 * 512:(par2 + 1) * 512],
                        xn_fm.ap(c * 2048 + tt * 128, c * 2048 + tt * 128 + 128),
                        wq_buf[buf].ap(c * 512, (c + 1) * 512),
                        start=(c == 0), stop=(c == NC - 1)), inc=inc)
                p.wait("vector", "t", last_t)
                dst3 = v_tm.ap().rearrange("p (t h j) -> p t h j", t=NT, h=H)[:, tt, hf * 8:(hf + 1) * 8, 0:64]
                src3 = PB[:, par2 * 512:(par2 + 1) * 512].rearrange("p (h j) -> p h j", h=8)
                bias3 = bias = b_v_bc.ap(hf * 512, (hf + 1) * 512).rearrange("p (h j) -> p h j", h=8)
                v_evac_v[par2] = p.run("vector", lambda En, d=dst3, s=src3, b3=bias3: En.tensor_tensor(
                    d, s, b3, op=AL.add), inc="v", n=1)
            wq_used_t[j] = last_t
        v_done_v = p.tok("v")

        router_v = 0
        p.wait("tensor", "gd", init_gd)
        for tt in range(NT):
            p.wait("tensor", "sc", qk_sc_done)
            p.wait("tensor", "v", router_v)
            for c in range(NC):
                inc = "t" if c == NC - 1 else None
                last_t = p.run("tensor", lambda En, c=c, tt=tt: En.matmul(
                    PA[:, 0:8],
                    normed_fm.ap(c * 2048 + tt * 128, c * 2048 + tt * 128 + 128),
                    rt_sb.ap(c * 8, (c + 1) * 8),
                    start=(c == 0), stop=(c == NC - 1)), inc=inc)
            p.wait("vector", "t", last_t)
            p.run("vector", lambda En: En.tensor_copy(lg_s.ap(), PA[:, 0:8]))
            p.drain("vector")
            p.run("vector", lambda En: En.max(mx8.ap(), lg_s.ap()))
            p.drain("vector")
            p.run("vector", lambda En: En.max_index(mi8.ap(), mx8.ap(), lg_s.ap()))
            p.drain("vector")
            p.run("vector", lambda En, tt=tt: En.tensor_copy(idx_cols.ap(tt, tt + 1), mi8.ap(0, 1)))
            p.run("vector", lambda En, tt=tt: En.tensor_copy(idx_cols.ap(NT + tt, NT + tt + 1), mi8.ap(1, 2)))
            p.drain("vector")
            d21_tok = p.runD("vector", lambda En: En.tensor_tensor(
                tcol.ap(9, 10), mx8.ap(1, 2), mx8.ap(0, 1), op=AL.subtract), inc="v", n=1)
            p.wait("scalar", "v", d21_tok)
            ee_tok = p.runD("scalar", lambda En: En.activation(
                tcol.ap(10, 11), tcol.ap(9, 10), ACT.Exp), inc="sc", n=1)
            p.wait("vector", "sc", ee_tok)
            p.run("vector", lambda En, tt=tt: En.tensor_scalar(
                m1m2.ap(0, 8), io8.ap(), idx_cols.ap(tt, tt + 1), None, op0=AL.is_equal))
            p.run("vector", lambda En, tt=tt: En.tensor_scalar(
                m1m2.ap(8, 16), io8.ap(), idx_cols.ap(NT + tt, NT + tt + 1), None, op0=AL.is_equal))
            p.run("vector", lambda En: En.tensor_scalar(tcol.ap(11, 12), tcol.ap(10, 11), 1.0, None,
                                                        op0=AL.add))
            p.drain("vector")
            p.run("vector", lambda En, tt=tt: En.reciprocal(wA_col.ap(tt, tt + 1), tcol.ap(11, 12)))
            p.drain("vector")
            p.run("vector", lambda En, tt=tt: En.tensor_scalar(
                wB_col.ap(tt, tt + 1), wA_col.ap(tt, tt + 1), -1.0, 1.0, op0=AL.mult, op1=AL.add))
            router_v = p.run("vector", lambda En, tt=tt: En.tensor_tensor(
                c_coef.ap(tt * 8, (tt + 1) * 8), m1m2.ap(0, 8), m1m2.ap(8, 16), op=AL.add),
                inc="v", n=1)
        router_done_v = router_v

        pv_free_v = [0, 0]
        exp_sc = {0: 0, 1: 0}
        pv_t = {}
        pT_used_t = [0, 0, 0, 0]
        attn_last_v = 0
        first_st = True
        NG = 8

        for cp in range(H // 2):
            for qb in range(4):
                st_t = {}

                def do_st(slot, g):
                    nonlocal first_st
                    h = 2 * cp + slot
                    r0 = 64 * slot
                    p.wait("tensor", "sc", exp_sc[slot])
                    if first_st:
                        p.wait("tensor", "v", router_done_v)
                        first_st = False
                    tok = 0
                    for kk in range(2):
                        kt = 2 * g + kk
                        bank = 2 * slot + kk
                        inc = "t" if kk == 1 else None
                        tok = p.run("tensor", lambda En, kk=kk, kt=kt, bank=bank, cp=cp, r0=r0, qb=qb: En.matmul(
                            PA[:, bank * 512:(bank + 1) * 512],
                            k_fm.ap(cp * 2048 + kt * 128, cp * 2048 + kt * 128 + 128, r0, r0 + 64),
                            q_fm.ap(cp * 2048 + qb * 512, cp * 2048 + (qb + 1) * 512, r0, r0 + 64),
                            start=True, stop=True), inc=inc)
                    st_t[(slot, g)] = tok

                def do_exp(slot, g):
                    buf = 2 * slot + (g % 2)
                    p.wait("scalar", "t", st_t[(slot, g)])
                    p.wait("scalar", "t", pT_used_t[buf])
                    exp_sc[slot] = p.run("scalar", lambda En, slot=slot, buf=buf: En.activation(
                        pT[buf].ap(), PA[:, slot * 1024: slot * 1024 + 1024],
                        ACT.Exp, bias=negm.ap(), scale=0.125), inc="sc", n=1)
                    exp_sc[(slot, g)] = exp_sc[slot]

                def do_pv(slot, g):
                    h = 2 * cp + slot
                    p.wait("tensor", "sc", exp_sc[(slot, g)])
                    if g == 0:
                        p.wait("tensor", "v", pv_free_v[slot])
                    buf = 2 * slot + (g % 2)
                    tok = 0
                    for kk in range(2):
                        for qt in range(4):
                            st_ = (g == 0 and kk == 0 and qt == 0)
                            sp_ = (g == NG - 1 and kk == 1 and qt == 3)
                            inc = "t" if (kk == 1 and qt == 3) else None
                            tok = p.run("tensor", lambda En, g=g, kk=kk, qt=qt, h=h, slot=slot,
                                        buf=buf, st_=st_, sp_=sp_: En.matmul(
                                PB[:, slot * 512 + qt * 65: slot * 512 + qt * 65 + 65],
                                pT[buf].ap(kk * 512 + qt * 128, kk * 512 + qt * 128 + 128),
                                v_tm.ap((2 * g + kk) * 1040 + h * 65, (2 * g + kk) * 1040 + (h + 1) * 65),
                                start=st_, stop=sp_, skip_group_check=True), inc=inc)
                    pv_t[(slot, g)] = tok
                    pT_used_t[buf] = tok

                for g in range(NG):
                    do_st(0, g)
                    do_st(1, g)
                    p.drain("tensor")
                    do_exp(0, g)
                    do_exp(1, g)
                    if g > 0:
                        do_pv(0, g - 1)
                        do_pv(1, g - 1)
                    p.drain("tensor")
                do_pv(0, NG - 1)
                do_pv(1, NG - 1)
                p.drain("tensor")

                for slot in range(2):
                    h = 2 * cp + slot
                    p.wait("vector", "t", pv_t[(slot, NG - 1)])
                    for qt in range(4):
                        p.run("vector", lambda En, qt=qt, slot=slot: En.reciprocal(
                            rec4.ap(qt, qt + 1), PB[:, slot * 512 + qt * 65 + 64: slot * 512 + qt * 65 + 65]))
                    p.drain("vector")
                    for qt in range(4):
                        gslot = qb * 4 + qt
                        inc = "v" if qt == 3 else None
                        attn_last_v = p.run("vector", lambda En, qt=qt, slot=slot, gslot=gslot, h=h: En.tensor_scalar(
                            O_tm.ap(gslot * 1024 + h * 64, gslot * 1024 + (h + 1) * 64),
                            PB[:, slot * 512 + qt * 65: slot * 512 + qt * 65 + 64],
                            rec4.ap(qt, qt + 1), None, op0=AL.mult), inc=inc)
                    pv_free_v[slot] = attn_last_v
        attn_done_v = p.tok("v")
        last_exp_sc = max(exp_sc[0], exp_sc[1])

        p.wait("gpsimd", "t", wq_used_t[5])
        p.wait("gpsimd", "v", rope_done_v)
        for c in range(NC):
            wproj_tok = p.run("gpsimd", lambda En, c=c: En.dma_start(
                out=wproj_buf.ap(c * 1024, (c + 1) * 1024),
                in_=wproj_d[c * 128:(c + 1) * 128, :]), inc="gd", n=16)

        otr_evac_v = {}
        otr_v = [0] * NT
        p.wait("tensor", "v", attn_done_v)
        p.wait("tensor", "sc", last_exp_sc)
        for g in range(NT):
            par = g % 2
            if par in otr_evac_v:
                p.wait("tensor", "v", otr_evac_v[par])
            pa_bf = PA[:, par * 512:(par + 1) * 512].bitcast(BF16)
            for c in range(NC):
                inc = "t" if c == NC - 1 else None
                otr_tok = p.run("tensor", lambda En, c=c, g=g, pa_bf=pa_bf: En.transpose(
                    pa_bf[:, c * 128:(c + 1) * 128],
                    O_tm.ap(g * 1024 + c * 128, g * 1024 + (c + 1) * 128), ident.ap()), inc=inc)
            p.wait("vector", "t", otr_tok)
            src3 = pa_bf.rearrange("p (c j) -> p c j", c=NC)
            dst3 = O_fm.ap(0, NC * T).rearrange("p (c t j) -> p c t j", c=NC, t=NT)[:, :, g, :]
            otr_v[g] = p.run("vector", lambda En, d=dst3, s=src3: En.tensor_copy(d, s), inc="v", n=1)
            otr_evac_v[par] = otr_v[g]

        proj_evac_v = {}
        p.wait("tensor", "gd", wproj_tok)
        for tt in range(NT):
            par = tt % 2
            p.wait("tensor", "v", otr_v[tt])
            if par in proj_evac_v:
                p.wait("tensor", "v", proj_evac_v[par])
            for c in range(NC):
                for dh2 in range(2):
                    inc = "t" if (c == NC - 1 and dh2 == 1) else None
                    proj_tok = p.run("tensor", lambda En, c=c, dh2=dh2, tt=tt, par=par: En.matmul(
                        PB[:, par * 1024 + dh2 * 512: par * 1024 + (dh2 + 1) * 512],
                        O_fm.ap(c * 2048 + tt * 128, c * 2048 + tt * 128 + 128),
                        wproj_buf.ap(c * 1024 + dh2 * 512, c * 1024 + (dh2 + 1) * 512),
                        start=(c == 0), stop=(c == NC - 1)), inc=inc)
            p.wait("vector", "t", proj_tok)
            proj_evac_v[par] = p.run("vector", lambda En, tt=tt, par=par: En.tensor_tensor(
                s_tm.ap(tt * 1024, (tt + 1) * 1024), PB[:, par * 1024:(par + 1) * 1024],
                b_proj_bc.ap(), op=AL.add), inc="v", n=1)
        proj_t_done = p.tok("t")

        colidx2 = r6(12288, 128, F32)
        triS = r6(12544, 128, F32)
        ones128 = r6(12800, 128, F32)
        sl_sc = r6(13120, 16, F32)
        au32 = r6(13152, 8, U32)
        S8 = r6(13168, 8, F32)
        msk8 = r6(13184, 16, F32)
        il_init = r6(15040, 48, U32)
        silu_e = [r6(13312, 768, BF16), r6(14080, 768, BF16)]
        tok_sb = r6(14848, 48, U32)
        dst_sb = r6(14944, 48, U32)

        p.wait("gpsimd", "v", rope_done_v)
        p.run("gpsimd", lambda En: En.iota(colidx2.ap(), [[1, 128]], channel_multiplier=0,
                                           allow_small_or_imprecise_dtypes=True), inc="gd", n=1)
        p.run("gpsimd", lambda En: En.iota(il_init.ap(), [[1, 48]], base=2 * T,
                                           channel_multiplier=48,
                                           allow_small_or_imprecise_dtypes=True), inc="gd", n=1)
        p.run("gpsimd", lambda En: En.dma_start(
            out=dstl_d[0:E * CAP].rearrange("(p k) -> p k", p=128), in_=il_init.ap()), inc="gd", n=16)
        p.run("gpsimd", lambda En: En.memset(il_init.ap(), 0), inc="gd", n=1)
        ini_tok = p.run("gpsimd", lambda En: En.dma_start(
            out=tokl_d[0:E * CAP].rearrange("(p k) -> p k", p=128), in_=il_init.ap()), inc="gd", n=16)
        p.wait("gpsimd", "gd", ini_tok)

        p.wait("vector", "gd", p.tok("gd"))
        p.run("vector", lambda En: En.memset(ones128.ap(), 1.0))
        rc_tok = p.runD("vector", lambda En: En.tensor_scalar(
            triS.ap(), colidx2.ap(), pidx.ap(0, 1), None, op0=AL.is_gt), inc="v", n=1)

        route_v = 0
        addr_gd = [0] * NT
        for tt in range(NT):
            p.wait("tensor", "v", max(rc_tok, router_done_v))
            p.wait("tensor", "v", route_v)
            for j in range(tt + 1):
                lhs = ones128 if j < tt else triS
                inc = "t" if j == tt else None
                pf_tok = p.run("tensor", lambda En, j=j, lhs=lhs: En.matmul(
                    PA[:, 0:8], lhs.ap(), c_coef.ap(j * 8, (j + 1) * 8),
                    start=(j == 0), stop=(j == tt)), inc=inc)
            p.wait("vector", "t", pf_tok)
            if tt >= 2:
                p.wait("vector", "gd", addr_gd[tt - 2])
            p.run("vector", lambda En: En.tensor_copy(S8.ap(), PA[:, 0:8]))
            p.drain("vector")
            for r in range(2):
                p.run("vector", lambda En, r=r, tt=tt: En.tensor_scalar(
                    msk8.ap(r * 8, r * 8 + 8), io8.ap(), idx_cols.ap(r * NT + tt, r * NT + tt + 1),
                    None, op0=AL.is_equal))
            p.drain("vector")
            for r in range(2):
                p.run("vector", lambda En, r=r: En.tensor_tensor(
                    msk8.ap(r * 8, r * 8 + 8), msk8.ap(r * 8, r * 8 + 8), S8.ap(), op=AL.mult))
            p.drain("vector")
            for r in range(2):
                p.run("vector", lambda En, r=r: En.reduce_sum(
                    sl_sc.ap(r, r + 1), msk8.ap(r * 8, r * 8 + 8), axis=mybir.AxisListType.X))
            p.drain("vector")
            for r in range(2):
                p.run("vector", lambda En, r=r: En.tensor_scalar_min(
                    sl_sc.ap(r, r + 1), sl_sc.ap(r, r + 1), float(CAP - 1)))
            p.drain("vector")
            for r in range(2):
                p.run("vector", lambda En, r=r: En.tensor_scalar(
                    sl_sc.ap(6 + r, 7 + r), sl_sc.ap(r, r + 1), 128.0, None, op0=AL.is_ge))
            for k in range(2, CT):
                for r in range(2):
                    p.run("vector", lambda En, r=r, k=k: En.tensor_scalar(
                        sl_sc.ap(8 + r, 9 + r), sl_sc.ap(r, r + 1), float(128 * k), None,
                        op0=AL.is_ge))
                p.drain("vector")
                for r in range(2):
                    p.run("vector", lambda En, r=r: En.tensor_tensor(
                        sl_sc.ap(6 + r, 7 + r), sl_sc.ap(6 + r, 7 + r), sl_sc.ap(8 + r, 9 + r),
                        op=AL.add))
                p.drain("vector")
            for r in range(2):
                p.run("vector", lambda En, r=r: En.tensor_scalar(
                    sl_sc.ap(10 + r, 11 + r), sl_sc.ap(6 + r, 7 + r), -128.0, None, op0=AL.mult))
            p.drain("vector")
            for r in range(2):
                p.run("vector", lambda En, r=r: En.tensor_tensor(
                    sl_sc.ap(10 + r, 11 + r), sl_sc.ap(r, r + 1), sl_sc.ap(10 + r, 11 + r),
                    op=AL.add))
            p.drain("vector")
            for r in range(2):
                p.run("vector", lambda En, r=r: En.tensor_scalar(
                    sl_sc.ap(2 + r, 3 + r), sl_sc.ap(10 + r, 11 + r), 48.0, None, op0=AL.mult))
                p.run("vector", lambda En, r=r, tt=tt: En.tensor_scalar(
                    sl_sc.ap(8 + r, 9 + r), idx_cols.ap(r * NT + tt, r * NT + tt + 1),
                    float(CT), None, op0=AL.mult))
            p.drain("vector")
            for r in range(2):
                p.run("vector", lambda En, r=r: En.tensor_tensor(
                    sl_sc.ap(2 + r, 3 + r), sl_sc.ap(2 + r, 3 + r), sl_sc.ap(8 + r, 9 + r),
                    op=AL.add))
            p.drain("vector")
            for r in range(2):
                p.run("vector", lambda En, r=r: En.tensor_tensor(
                    sl_sc.ap(2 + r, 3 + r), sl_sc.ap(2 + r, 3 + r), sl_sc.ap(6 + r, 7 + r),
                    op=AL.add))
            p.run("vector", lambda En, tt=tt: En.tensor_scalar(
                sl_sc.ap(4, 5), pidx.ap(0, 1), 1.0, float(tt * 128), op0=AL.mult, op1=AL.add))
            p.drain("vector")
            p.run("vector", lambda En: En.tensor_scalar(
                sl_sc.ap(5, 6), sl_sc.ap(4, 5), 1.0, float(T), op0=AL.mult, op1=AL.add))
            p.drain("vector")
            p.run("vector", lambda En: En.tensor_copy(au32.ap(0, 1), sl_sc.ap(2, 3)))
            p.run("vector", lambda En: En.tensor_copy(au32.ap(1, 2), sl_sc.ap(3, 4)))
            p.run("vector", lambda En: En.tensor_copy(au32.ap(2, 3), sl_sc.ap(4, 5)))
            route_v = p.runD("vector", lambda En: En.tensor_copy(au32.ap(3, 4), sl_sc.ap(5, 6)),
                             inc="v", n=1)
            p.wait("gpsimd", "v", route_v)
            p.run("gpsimd", lambda En: En.indirect_dma_start(
                out=tokl_d.rearrange("(a b) -> a b", b=1), out_offset=bass.IndirectOffsetOnAxis(au32.ap(0, 1), 0),
                in_=au32.ap(2, 3), in_offset=None), inc="gd", n=16)
            p.run("gpsimd", lambda En: En.indirect_dma_start(
                out=tokl_d.rearrange("(a b) -> a b", b=1), out_offset=bass.IndirectOffsetOnAxis(au32.ap(1, 2), 0),
                in_=au32.ap(2, 3), in_offset=None), inc="gd", n=16)
            p.run("gpsimd", lambda En: En.indirect_dma_start(
                out=dstl_d.rearrange("(a b) -> a b", b=1), out_offset=bass.IndirectOffsetOnAxis(au32.ap(0, 1), 0),
                in_=au32.ap(2, 3), in_offset=None), inc="gd", n=16)
            addr_gd[tt] = p.run("gpsimd", lambda En: En.indirect_dma_start(
                out=dstl_d.rearrange("(a b) -> a b", b=1), out_offset=bass.IndirectOffsetOnAxis(au32.ap(1, 2), 0),
                in_=au32.ap(3, 4), in_offset=None), inc="gd", n=16)
        p.wait("gpsimd", "gd", addr_gd[NT - 1])
        p.run("gpsimd", lambda En: En.dma_start(
            out=tok_sb.ap(), in_=tokl_d[0:E * CAP].rearrange("(p k) -> p k", p=128)), inc="gd", n=16)
        rb_tok = p.run("gpsimd", lambda En: En.dma_start(
            out=dst_sb.ap(), in_=dstl_d[0:E * CAP].rearrange("(p k) -> p k", p=128)), inc="gd", n=16)

        xe_tm = [Buf(arena, REG1.off + st * 1024, D, BF16) for st in range(CT)]
        out_e = [Buf(arena, REG1.off + 6144 + st * 1024, D, BF16) for st in range(CT)]
        gup_sp = [Buf(arena, REG1.off + 12288 + i * 1024, D, BF16) for i in range(4)]
        hid_sp = [Buf(arena, REG4.off + i * 768, CAP, BF16) for i in range(21)] + \
                 [Buf(arena, REG5.off, CAP, BF16)]
        xe_fm = Buf(arena, REG5.off + 768, NC * CAP, BF16)
        dn_sp = [Buf(arena, REG5.off + 6912 + k * 1024, D, BF16) for k in range(9)] + \
                [Buf(arena, REG6.off + (k - 9) * 1024, D, BF16) for k in range(9, HT)]

        gup_slot_t = [0] * 4
        dn_used_t = 0
        xe_tr_t = 0
        out_sc_gd = [0] * CT
        xfm_ev = {}
        sil_used = [0, 0]
        gup_free = [0, 0]
        hid_used_t = 0
        dn_ev = {}
        first_ffn_g = True
        nrm_sd_all = p.tok("sd")

        for e in range(E):
            if first_ffn_g:
                p.wait("gpsimd", "gd", rb_tok)
                p.wait("gpsimd", "sd", nrm_sd_all)
                p.wait("gpsimd", "t", otr_tok)
                p.wait("gpsimd", "v", attn_done_v)
            p.wait("gpsimd", "t", xe_tr_t)
            g_toks = []
            for st in range(CT):
                g_toks.append(p.run("gpsimd", lambda En, e=e, st=st: En.indirect_dma_start(
                    out=xe_tm[st].ap(), out_offset=None, in_=nrm_d,
                    in_offset=bass.IndirectOffsetOnAxis(tok_sb.ap(e * CT + st, e * CT + st + 1), 0)),
                    inc="gd", n=16))

            xfm_all = p.tok("v")
            for st in range(CT):
                par = st % 2
                p.wait("tensor", "gd", g_toks[st])
                if par in xfm_ev:
                    p.wait("tensor", "v", xfm_ev[par])
                pa_bf = PA[:, par * 512:(par + 1) * 512].bitcast(BF16)
                for c in range(NC):
                    inc = "t" if c == NC - 1 else None
                    xe_tr_t = p.run("tensor", lambda En, c=c, st=st, pa_bf=pa_bf: En.transpose(
                        pa_bf[:, c * 128:(c + 1) * 128], xe_tm[st].ap(c * 128, (c + 1) * 128),
                        ident.ap()), inc=inc)
                p.wait("vector", "t", xe_tr_t)
                dst3 = xe_fm.ap().rearrange("p (c j) -> p c j", c=NC)[:, :, st * 128:(st + 1) * 128]
                src3 = pa_bf.rearrange("p (c j) -> p c j", c=NC)
                xfm_ev[par] = p.run("vector", lambda En, d=dst3, s=src3: En.tensor_copy(d, s),
                                    inc="v", n=1)
            xfm_all = p.tok("v")

            def load_gup_sp(i):
                sg_, su_ = (2 * i) % 4, (2 * i + 1) % 4
                p.wait("gpsimd", "t", gup_slot_t[sg_])
                p.run("gpsimd", lambda En, e=e, i=i, sg_=sg_: En.dma_start(
                    out=gup_sp[sg_].ap(), in_=gup_d[e, i]), inc="gd", n=16)
                p.wait("gpsimd", "t", gup_slot_t[su_])
                return p.run("gpsimd", lambda En, e=e, i=i, su_=su_: En.dma_start(
                    out=gup_sp[su_].ap(), in_=gup_d[e, HT + i]), inc="gd", n=16)

            if first_ffn_g:
                p.wait("gpsimd", "t", proj_t_done)
                p.wait("gpsimd", "sc", last_exp_sc)
                first_ffn_g = False
            gu_toks = {}
            gu_toks[0] = load_gup_sp(0)
            gu_toks[1] = load_gup_sp(1)
            dn_toks = []
            for k in range(HT):
                p.wait("gpsimd", "t", dn_used_t)
                dn_toks.append(p.run("gpsimd", lambda En, e=e, k=k: En.dma_start(
                    out=dn_sp[k].ap(), in_=down_d[e, k]), inc="gd", n=16))

            PSET = [PA, PB]
            for i in range(HT):
                ps = PSET[i % 2]
                sg_, su_ = (2 * i) % 4, (2 * i + 1) % 4
                p.wait("tensor", "gd", gu_toks[i])
                p.wait("tensor", "v", max(xfm_all, gup_free[i % 2]))
                for half, (n0, n1) in enumerate(((0, 512), (512, 768))):
                    for c in range(NC):
                        inc = "t" if (half == 1 and c == NC - 1) else None
                        gmm = p.run("tensor", lambda En, c=c, n0=n0, n1=n1, ps=ps, sg_=sg_: En.matmul(
                            ps[:, n0:n1], gup_sp[sg_].ap(c * 128, (c + 1) * 128),
                            xe_fm.ap().rearrange("p (c j) -> p c j", c=NC)[:, c, n0:n1],
                            start=(c == 0), stop=(c == NC - 1)), inc=inc)
                for half, (n0, n1) in enumerate(((1024, 1536), (1536, 1792))):
                    for c in range(NC):
                        inc = "t" if (half == 1 and c == NC - 1) else None
                        umm = p.run("tensor", lambda En, c=c, n0=n0, n1=n1, ps=ps, su_=su_: En.matmul(
                            ps[:, n0:n1], gup_sp[su_].ap(c * 128, (c + 1) * 128),
                            xe_fm.ap().rearrange("p (c j) -> p c j", c=NC)[:, c, n0 - 1024:n1 - 1024],
                            start=(c == 0), stop=(c == NC - 1)), inc=inc)
                p.wait("scalar", "t", gmm)
                p.wait("scalar", "v", sil_used[i % 2])
                sil = p.run("scalar", lambda En, i=i, ps=ps: En.activation(
                    silu_e[i % 2].ap(), ps[:, 0:768], ACT.Silu), inc="sc", n=1)
                p.wait("vector", "sc", sil)
                p.wait("vector", "t", umm)
                if i < 2:
                    p.wait("vector", "t", hid_used_t)
                hv = p.run("vector", lambda En, i=i, ps=ps: En.tensor_tensor(
                    hid_sp[i].ap(), silu_e[i % 2].ap(), ps[:, 1024:1792], op=AL.mult), inc="v", n=1)
                sil_used[i % 2] = hv
                gup_free[i % 2] = hv
                gup_slot_t[sg_] = umm
                gup_slot_t[su_] = umm
                if i + 2 < HT:
                    gu_toks[i + 2] = load_gup_sp(i + 2)
            hid_all = p.tok("v")

            for st in range(CT):
                ps = PSET[st % 2]
                p.wait("tensor", "v", hid_all)
                p.wait("tensor", "gd", dn_toks[HT - 1])
                if st % 2 in dn_ev:
                    p.wait("tensor", "v", dn_ev[st % 2])
                for k in range(HT):
                    for dh2 in range(2):
                        st_ = (k == 0)
                        sp_ = (k == HT - 1)
                        inc = "t" if (k == HT - 1 and dh2 == 1) else None
                        dmm = p.run("tensor", lambda En, k=k, dh2=dh2, st=st, ps=ps,
                                    st_=st_, sp_=sp_: En.matmul(
                            ps[:, dh2 * 512:(dh2 + 1) * 512],
                            hid_sp[k].ap(st * 128, (st + 1) * 128),
                            dn_sp[k].ap(dh2 * 512, (dh2 + 1) * 512),
                            start=st_, stop=sp_), inc=inc)
                p.wait("vector", "t", dmm)
                p.wait("vector", "gd", out_sc_gd[st])
                dn_ev[st % 2] = p.run("vector", lambda En, st=st, ps=ps: En.tensor_copy(
                    out_e[st].ap(), ps[:, 0:1024]), inc="v", n=1)
                p.wait("gpsimd", "v", dn_ev[st % 2])
                out_sc_gd[st] = p.run("gpsimd", lambda En, e=e, st=st: En.indirect_dma_start(
                    out=moeab_d, out_offset=bass.IndirectOffsetOnAxis(
                        dst_sb.ap(e * CT + st, e * CT + st + 1), 0),
                    in_=out_e[st].ap(), in_offset=None), inc="gd", n=16)
            dn_used_t = dmm
            hid_used_t = dmm
        ffn_t_done = p.tok("t")
        ffn_sc_done = p.tok("sc")
        moe_gd_all = p.tok("gd")

        moab_t = [r6(12288 + k * 1024, D, BF16) for k in range(4)]
        fin_v = [0] * NT
        store_sd = [0] * NT
        p.wait("sync", "t", ffn_t_done)
        p.wait("sync", "sc", ffn_sc_done)
        p.wait("sync", "gd", moe_gd_all)
        p.wait("sync", "v", p.tok("v"))
        for tt in range(NT):
            par = tt % 2
            if tt >= 2:
                p.wait("sync", "v", fin_v[tt - 2])
            x_tok = p.run("sync", lambda En, tt=tt, par=par: En.dma_start(
                out=fx[par].ap(), in_=x_d[tt * 128:(tt + 1) * 128, :]), inc="sd", n=16)
            p.run("sync", lambda En, tt=tt, par=par: En.dma_start(
                out=moab_t[2 * par].ap(), in_=moeab_d[tt * 128:(tt + 1) * 128, :]), inc="sd", n=16)
            x_tok = p.run("sync", lambda En, tt=tt, par=par: En.dma_start(
                out=moab_t[2 * par + 1].ap(), in_=moeab_d[T + tt * 128:T + (tt + 1) * 128, :]),
                inc="sd", n=16)
            p.wait("vector", "sd", x_tok)
            p.run("vector", lambda En, tt=tt, par=par: En.tensor_scalar(
                fscr1.ap(), fx[par].ap(), r_col.ap(tt, tt + 1), mr_col.ap(tt, tt + 1),
                op0=AL.mult, op1=AL.add))
            p.run("vector", lambda En, tt=tt, par=par: En.tensor_scalar(
                fscr2.ap(), moab_t[2 * par].ap(), wA_col.ap(tt, tt + 1), None, op0=AL.mult))
            p.drain("vector")
            p.run("vector", lambda En, tt=tt, par=par: En.scalar_tensor_tensor(
                fscr2.ap(), moab_t[2 * par + 1].ap(), wB_col.ap(tt, tt + 1), fscr2.ap(),
                op0=AL.mult, op1=AL.add))
            p.drain("vector")
            ffb = p.run("vector", lambda En: En.scalar_tensor_tensor(
                fscr2.ap(), fscr2.ap(), alpha_s.ap(), fscr1.ap(),
                op0=AL.mult, op1=AL.add), inc="v", n=1)
            p.wait("scalar", "v", ffb)
            g_tok = p.run("scalar", lambda En: En.activation(
                fscr1.ap(), fscr2.ap(), ACT.Silu), inc="sc", n=1)
            p.wait("vector", "sc", g_tok)
            p.run("vector", lambda En, tt=tt: En.tensor_tensor(
                fscr2.ap(), s_tm.ap(tt * 1024, (tt + 1) * 1024), fscr1.ap(), op=AL.mult))
            if tt >= 2:
                p.wait("vector", "sd", store_sd[tt - 2])
            fin_v[tt] = p.run("vector", lambda En, par=par: En.tensor_tensor(
                fout[par].ap(), fx[par].ap(), fscr2.ap(), op=AL.add), inc="v", n=1)
            p.wait("sync", "v", fin_v[tt])
            store_sd[tt] = p.run("sync", lambda En, tt=tt, par=par: En.dma_start(
                out=y_d[tt * 128:(tt + 1) * 128, :], in_=fout[par].ap()), inc="sd", n=16)


        p.wait("sync", "sd", p.tok("sd"))

        @block.tensor
        def _(En):
            for f in p.prog["tensor"]:
                f(En)

        @block.vector
        def _(En):
            for f in p.prog["vector"]:
                f(En)

        @block.scalar
        def _(En):
            for f in p.prog["scalar"]:
                f(En)

        @block.sync
        def _(En):
            for f in p.prog["sync"]:
                f(En)

        @block.gpsimd
        def _(En):
            for f in p.prog["gpsimd"]:
                f(En)

    return nc


_NC_CACHE = None


def kernel(**inputs):
    global _NC_CACHE
    x = np.ascontiguousarray(np.asarray(inputs["L0_attn_x"], dtype=np.float32))
    sin = np.asarray(inputs["L0_attn_sin"], dtype=np.float32)
    cos = np.asarray(inputs["L0_attn_cos"], dtype=np.float32)
    w_qkv = np.ascontiguousarray(np.asarray(inputs["L0_attn_w_qkv"], dtype=np.float32))
    b_qkv = np.asarray(inputs["L0_attn_b_qkv"], dtype=np.float32)
    w_proj = np.ascontiguousarray(np.asarray(inputs["L0_attn_w_proj"], dtype=np.float32))
    b_proj = np.asarray(inputs["L0_attn_b_proj"], dtype=np.float32)
    nw = np.asarray(inputs["L0_ffn_norm_weight"], dtype=np.float32)
    rw = np.asarray(inputs["L0_ffn_router_weight"], dtype=np.float32)
    gup_w = np.asarray(inputs["L0_ffn_gate_up_weight"], dtype=np.float32)
    down_w = np.asarray(inputs["L0_ffn_down_weight"], dtype=np.float32)
    alpha = np.asarray(inputs["L0_ffn_residual_alpha"], dtype=np.float32)

    sin_t = np.ascontiguousarray(np.tile(sin.T, (2, 1)))
    cos_t = np.ascontiguousarray(np.tile(cos.T, (2, 1)))
    b_qkv_pp = np.ascontiguousarray(b_qkv.reshape(24, 128).T)
    b_v_bc = np.ascontiguousarray(np.tile(b_qkv[2 * D:3 * D][None, :], (128, 1)))
    b_proj_bc = np.ascontiguousarray(np.tile(b_proj[None, :], (128, 1)))
    nw_bc = np.ascontiguousarray(np.tile(nw[None, :], (128, 1)))
    rt4 = np.ascontiguousarray(rw.T.reshape(NC, 128, E))
    gup = np.ascontiguousarray(
        gup_w.reshape(E, 2 * HT, 128, NC, 128).transpose(0, 1, 4, 3, 2).reshape(E, 2 * HT, 128, D))
    down = np.ascontiguousarray(
        down_w.transpose(0, 2, 1).reshape(E, HT, 128, D))
    alpha_t = np.ascontiguousarray(np.tile(alpha.reshape(1, 1), (128, 1)))

    if _NC_CACHE is None:
        _NC_CACHE = build_nc()
    nc = _NC_CACHE

    shared = dict(sin_t=sin_t, cos_t=cos_t, w_qkv=w_qkv, b_qkv_pp=b_qkv_pp,
                  b_v_bc=b_v_bc, w_proj=w_proj, b_proj_bc=b_proj_bc, nw_bc=nw_bc,
                  rt4=rt4, gup=gup, down=down, alpha_t=alpha_t)
    in_maps = [dict(x=np.ascontiguousarray(x[b]), **shared) for b in range(8)]
    res = run_bass_kernel_spmd(nc, in_maps, core_ids=list(range(8)))
    return np.stack([r["y"] for r in res.results], axis=0)


# version 07 (final)
# speedup vs baseline: 1.4095x; 1.0168x over previous
import numpy as np
import concourse.bass as bass
import concourse.mybir as mybir
from concourse.alu_op_type import AluOpType
from concourse.bass_utils import run_bass_kernel_spmd
from contextlib import ExitStack

F32 = mybir.dt.float32
BF16 = mybir.dt.bfloat16
U32 = mybir.dt.uint32
ACT = mybir.ActivationFunctionType
AL = AluOpType

T = 2048
D = 1024
H = 16
DH = 64
HID = 2816
E = 8
NR = 1369
PRE = T - NR
NT = T // 128
NC = D // 128
HT = HID // 128
M_SOFT = 32.0
CAP = 768
CT = CAP // 128

ENGINES = ("tensor", "vector", "scalar", "sync", "gpsimd")


class Plan:

    def __init__(self, nc, sem_names):
        self.nc = nc
        self.prog = {e: [] for e in ENGINES}
        self.count = {s: 0 for s in sem_names}
        self.handles = {}
        self.waited = {e: {s: 0 for s in sem_names} for e in ENGINES}

    def wait(self, eng, sem, val=None):
        v = self.count[sem] if val is None else val
        if v is None or v <= 0 or v <= self.waited[eng][sem]:
            return
        self.waited[eng][sem] = v
        h = self.handles[sem]
        self.prog[eng].append(lambda En, h=h, v=v: En.wait_ge(h, v))

    def run(self, eng, fn, inc=None, n=1):
        if inc is None:
            self.prog[eng].append(fn)
            return self.count.get("_", 0)
        h = self.handles[inc]
        self.count[inc] += n
        self.prog[eng].append(lambda En, fn=fn, h=h, n=n: fn(En).then_inc(h, n))
        return self.count[inc]

    def tok(self, sem):
        return self.count[sem]

    def drain(self, eng):
        self.prog[eng].append(lambda En: En.drain())

    def runD(self, eng, fn, inc, n=1):
        self.prog[eng].append(fn)
        h = self.handles[inc]
        self.count[inc] += n
        self.prog[eng].append(lambda En, h=h, n=n: En.drain().then_inc(h, n))
        return self.count[inc]


class Buf:

    def __init__(self, arena, off_units, width_elems, dtype):
        self.arena = arena
        self.off = off_units
        self.width = width_elems
        self.dtype = dtype
        self.esz = 1 if dtype == BF16 else 2

    def ap(self, c0=0, c1=None, r0=0, r1=128):
        if c1 is None:
            c1 = self.width
        a = self.arena[r0:r1, self.off + c0 * self.esz: self.off + c1 * self.esz]
        if self.dtype != BF16:
            a = a.bitcast(self.dtype)
        return a


def build_nc():
    nc = bass.Bass("TRN2", target_bir_lowering=False)
    x_d = nc.dram_tensor("x", [T, D], F32, kind="ExternalInput").ap()
    sin_d = nc.dram_tensor("sin_t", [128, NR], F32, kind="ExternalInput").ap()
    cos_d = nc.dram_tensor("cos_t", [128, NR], F32, kind="ExternalInput").ap()
    wqkv_d = nc.dram_tensor("w_qkv", [D, 3 * D], F32, kind="ExternalInput").ap()
    bqkv_d = nc.dram_tensor("b_qkv_pp", [128, 24], F32, kind="ExternalInput").ap()
    bv_d = nc.dram_tensor("b_v_bc", [128, D], F32, kind="ExternalInput").ap()
    wproj_d = nc.dram_tensor("w_proj", [D, D], F32, kind="ExternalInput").ap()
    bproj_d = nc.dram_tensor("b_proj_bc", [128, D], F32, kind="ExternalInput").ap()
    nw_d = nc.dram_tensor("nw_bc", [128, D], F32, kind="ExternalInput").ap()
    rt_d = nc.dram_tensor("rt4", [NC, 128, E], F32, kind="ExternalInput").ap()
    gup_d = nc.dram_tensor("gup", [E, 2 * HT, 128, D], F32, kind="ExternalInput").ap()
    down_d = nc.dram_tensor("down", [E, HT, 128, D], F32, kind="ExternalInput").ap()
    alpha_d = nc.dram_tensor("alpha_t", [128, 1], F32, kind="ExternalInput").ap()
    y_d = nc.dram_tensor("y", [T, D], F32, kind="ExternalOutput").ap()
    nrm_d = nc.dram_tensor("nrm_sc", [T, D], BF16, kind="Internal").ap()
    tokl_d = nc.dram_tensor("tok_l", [E * CAP + 256], U32, kind="Internal").ap()
    dstl_d = nc.dram_tensor("dst_l", [E * CAP + 256], U32, kind="Internal").ap()
    moeab_d = nc.dram_tensor("moe_ab", [2 * T + E * CAP, D], BF16, kind="Internal").ap()

    with ExitStack() as ctx:
        en = ctx.enter_context
        arena_h = en(nc.sbuf_tensor("arena", [128, 106240], BF16))
        arena = arena_h[:]
        PA = en(nc.psum_tensor("PA", [128, 2048], F32))
        PB = en(nc.psum_tensor("PB", [128, 2048], F32))

        sems = ["t", "v", "sc", "gd", "sd"]
        p = Plan(nc, sems)
        for s in sems:
            p.handles[s] = en(nc.semaphore(s))
        block = en(nc.Block())

        off = [0]

        def mk(width_elems, dtype, align=16):
            units = width_elems * (1 if dtype == BF16 else 2)
            off[0] = (off[0] + align - 1) // align * align
            b = Buf(arena, off[0], width_elems, dtype)
            off[0] += units
            return b

        ident = mk(128, BF16)
        R2 = mk(128, BF16)
        io8 = mk(E, F32)
        alpha_s = mk(1, F32)
        b_qkv_pp = mk(24, F32)
        r_col = mk(NT, F32)
        mr_col = mk(NT, F32)
        rr2_col = mk(NT, F32)
        mrr2_col = mk(NT, F32)
        tcol = mk(16, F32)
        lg_s = mk(E, F32)
        mx8 = mk(E, F32)
        mi8 = mk(E, U32)
        idxf = mk(2, F32)
        w12 = mk(2, F32)
        m1m2 = mk(2 * E, F32)
        c_coef = mk(128, F32)
        pidx = mk(4, F32)
        rec4 = mk(4, F32)
        negm = mk(1, F32)
        wA_col = mk(NT, F32)
        wB_col = mk(NT, F32)
        b_v_bc = mk(D, F32)
        b_proj_bc = mk(D, F32)
        nw_bc = mk(D, F32)
        rt_sb = mk(NC * E, BF16)

        REG1 = mk(16384, BF16, 256)
        REG2 = mk(16384, BF16, 256)
        REG3 = mk(16384, BF16, 256)
        REG4 = mk(16640, BF16, 256)
        REG5 = mk(16384, BF16, 256)
        REG6 = mk(16384, BF16, 256)
        assert off[0] <= 106240, off[0]

        xn_fm, O_tm = REG1, REG1
        q_fm, s_tm = REG2, REG2
        k_fm, moe = REG3, REG3
        v_tm, O_fm = REG4, REG4
        normed_fm = REG5

        def r6(off_u, width_elems, dtype):
            return Buf(arena, REG6.off + off_u, width_elems, dtype)

        xld = [r6(0, D, F32), r6(2048, D, F32)]
        sqb = r6(4096, D, F32)
        xn_stage = r6(6144, D, BF16)
        normed_pre = r6(7168, D, F32)
        normed_stage = r6(9216, D, BF16)
        sin_sb = r6(10256, NR, F32)
        cos_sb = r6(13024, NR, F32)
        wq_buf = [r6(0, 4096, BF16), r6(4096, 4096, BF16)]
        rope_s1 = r6(8192, 512, F32)
        rope_s2 = r6(9232, 512, F32)
        pT = [r6(0, 1024, BF16), r6(1024, 1024, BF16), r6(2048, 1024, BF16), r6(3072, 1024, BF16)]
        wproj_buf = r6(4096, NC * D, BF16)
        gup_buf = [r6(i * 1024, D, BF16) for i in range(4)]
        down_buf = [r6(4096 + i * 1024, D, BF16) for i in range(11)]
        silu_st = [r6(15360, 512, BF16), r6(15872, 512, BF16)]
        colidx = r6(8192, 128, F32)
        scrA = r6(8448, 128, F32)
        idx_cols = r6(13056, 32, F32)
        fx = [r6(0, D, F32), r6(2048, D, F32)]
        fscr1 = r6(4096, D, F32)
        fscr2 = r6(6144, D, F32)
        fout = [r6(8192, D, F32), r6(10240, D, F32)]

        for dst, src in [(b_qkv_pp.ap(), bqkv_d), (nw_bc.ap(), nw_d),
                         (b_v_bc.ap(), bv_d), (b_proj_bc.ap(), bproj_d),
                         (alpha_s.ap(), alpha_d), (sin_sb.ap(), sin_d),
                         (cos_sb.ap(), cos_d)]:
            init_sd = p.run("sync", lambda En, d=dst, s=src: En.dma_start(out=d, in_=s), inc="sd", n=16)

        p.run("gpsimd", lambda En: En.iota(io8.ap(), [[1, E]], channel_multiplier=0,
                                           allow_small_or_imprecise_dtypes=True), inc="gd", n=1)
        p.run("gpsimd", lambda En: En.iota(colidx.ap(), [[1, 128]], channel_multiplier=0,
                                           allow_small_or_imprecise_dtypes=True), inc="gd", n=1)
        p.run("gpsimd", lambda En: En.iota(pidx.ap(0, 1), [[0, 1]], channel_multiplier=1,
                                           allow_small_or_imprecise_dtypes=True), inc="gd", n=1)
        vone = v_tm.ap().rearrange("p (t h j) -> p t h j", t=NT, h=H)[:, :, :, 64:65]
        p.run("gpsimd", lambda En: En.memset(vone, 1.0), inc="gd", n=1)
        p.run("gpsimd", lambda En: En.iota(
            scrA.ap().rearrange("p (z y x) -> p z y x", z=2, y=2), [[0, 2], [2, 2], [0, 32]],
            base=-1, channel_multiplier=0, allow_small_or_imprecise_dtypes=True), inc="gd", n=1)
        init_gd = p.run("gpsimd", lambda En: En.dma_start(
            out=rt_sb.ap().rearrange("p (c e) -> p c e", c=NC),
            in_=rt_d.rearrange("c p e -> p c e")), inc="gd", n=16)

        p.wait("sync", "sd", init_sd)
        p.wait("gpsimd", "gd", init_gd)
        p.wait("vector", "gd", init_gd)
        p.wait("vector", "sd", init_sd)
        p.run("vector", lambda En: En.tensor_scalar(ident.ap(), colidx.ap(), pidx.ap(0, 1), None,
                                                    op0=AL.is_equal))
        p.run("vector", lambda En: En.tensor_scalar(pidx.ap(1, 2), pidx.ap(0, 1), 32.0, None, op0=AL.add))
        p.run("vector", lambda En: En.tensor_scalar(pidx.ap(2, 3), pidx.ap(0, 1), -32.0, None, op0=AL.add))
        p.run("vector", lambda En: En.tensor_scalar(tcol.ap(0, 1), pidx.ap(0, 1), 64.0, None, op0=AL.is_ge))
        p.drain("vector")
        p.run("vector", lambda En: En.tensor_scalar(tcol.ap(1, 2), tcol.ap(0, 1), -64.0, None, op0=AL.mult))
        p.drain("vector")
        p.run("vector", lambda En: En.tensor_tensor(tcol.ap(1, 2), pidx.ap(0, 1), tcol.ap(1, 2), op=AL.add))
        p.drain("vector")
        p.run("vector", lambda En: En.tensor_scalar(tcol.ap(2, 3), tcol.ap(1, 2), 32.0, None, op0=AL.is_lt))
        p.drain("vector")
        p.run("vector", lambda En: En.tensor_scalar(tcol.ap(3, 4), tcol.ap(2, 3), -1.0, 1.0,
                                                    op0=AL.mult, op1=AL.add))
        p.drain("vector")
        p.run("vector", lambda En: En.tensor_scalar(c_coef.ap(), colidx.ap(), pidx.ap(1, 2), None,
                                                    op0=AL.is_equal))
        p.run("vector", lambda En: En.tensor_scalar(colidx.ap(), colidx.ap(), pidx.ap(2, 3), None,
                                                    op0=AL.is_equal))
        p.drain("vector")
        p.run("vector", lambda En: En.tensor_scalar(c_coef.ap(), c_coef.ap(), tcol.ap(2, 3), None,
                                                    op0=AL.mult))
        p.run("vector", lambda En: En.tensor_scalar(colidx.ap(), colidx.ap(), tcol.ap(3, 4), None,
                                                    op0=AL.mult))
        p.drain("vector")
        p.run("vector", lambda En: En.tensor_tensor(c_coef.ap(), c_coef.ap(), colidx.ap(), op=AL.add))
        p.drain("vector")
        p.run("vector", lambda En: En.tensor_tensor(R2.ap(), c_coef.ap(), scrA.ap(), op=AL.mult))
        p.run("vector", lambda En: En.memset(negm.ap(), -M_SOFT))
        p.run("vector", lambda En: En.tensor_scalar_min(alpha_s.ap(), alpha_s.ap(), 2.0))
        consts_v = p.run("vector", lambda En: En.tensor_scalar_max(alpha_s.ap(), alpha_s.ap(), -0.5),
                         inc="v", n=1)

        ln_v_done = [0] * NT
        ln_sc_done = [0] * NT
        evac_v = {}
        stage_t = 0
        sq_v = [0] * NT
        ln_mm_t = 0
        nrm_st_tok = 0

        for tt in range(NT):
            par = tt % 2
            if tt >= 2:
                p.wait("sync", "v", ln_v_done[tt - 2])
                p.wait("sync", "sc", ln_sc_done[tt - 2])
            x_tok = p.run("sync", lambda En, tt=tt, par=par: En.dma_start(
                out=xld[par].ap(), in_=x_d[tt * 128:(tt + 1) * 128, :]), inc="sd", n=16)

            p.wait("scalar", "sd", x_tok)
            if tt >= 1:
                p.wait("scalar", "v", sq_v[tt - 1])
            ln_sc_done[tt] = p.run("scalar", lambda En, par=par: En.activation(
                sqb.ap(), xld[par].ap(), ACT.Square), inc="sc", n=1)

            p.wait("vector", "sd", x_tok)
            p.wait("vector", "sc", ln_sc_done[tt])
            p.run("vector", lambda En, par=par: En.reduce_sum(tcol.ap(0, 1), xld[par].ap(),
                                                              axis=mybir.AxisListType.X))
            sq_v[tt] = p.run("vector", lambda En: En.reduce_sum(tcol.ap(1, 2), sqb.ap(),
                                                                axis=mybir.AxisListType.X), inc="v", n=1)
            p.drain("vector")
            p.run("vector", lambda En: En.tensor_scalar(tcol.ap(2, 3), tcol.ap(0, 1), 1.0 / D, None,
                                                        op0=AL.mult))
            p.run("vector", lambda En: En.tensor_scalar(tcol.ap(3, 4), tcol.ap(1, 2), 1.0 / D, None,
                                                        op0=AL.mult))
            p.drain("vector")
            p.run("vector", lambda En: En.tensor_tensor(tcol.ap(4, 5), tcol.ap(2, 3), tcol.ap(2, 3),
                                                        op=AL.mult))
            p.drain("vector")
            p.run("vector", lambda En: En.tensor_tensor(tcol.ap(3, 4), tcol.ap(3, 4), tcol.ap(4, 5),
                                                        op=AL.subtract))
            p.drain("vector")
            p.run("vector", lambda En: En.tensor_scalar(tcol.ap(4, 5), tcol.ap(3, 4), 1e-5, None,
                                                        op0=AL.add))
            p.drain("vector")
            p.run("vector", lambda En: En.reciprocal(tcol.ap(5, 6), tcol.ap(4, 5)))
            p.drain("vector")
            p.run("vector", lambda En: En.tensor_tensor(tcol.ap(6, 7), tcol.ap(3, 4), tcol.ap(5, 6),
                                                        op=AL.mult))
            p.drain("vector")
            p.run("vector", lambda En: En.tensor_scalar(tcol.ap(6, 7), tcol.ap(6, 7), 1e-6, None,
                                                        op0=AL.add))
            p.drain("vector")
            rv2_tok = p.runD("vector", lambda En: En.reciprocal(tcol.ap(7, 8), tcol.ap(6, 7)),
                             inc="v", n=1)
            p.wait("scalar", "v", rv2_tok)
            p.run("scalar", lambda En, tt=tt: En.activation(
                r_col.ap(tt, tt + 1), tcol.ap(5, 6), ACT.Sqrt))
            p.drain("scalar")
            r2_tok = p.runD("scalar", lambda En: En.activation(
                tcol.ap(8, 9), tcol.ap(7, 8), ACT.Sqrt), inc="sc", n=1)
            p.wait("vector", "sc", r2_tok)
            p.run("vector", lambda En, tt=tt: En.tensor_scalar(
                mr_col.ap(tt, tt + 1), tcol.ap(2, 3), r_col.ap(tt, tt + 1), -1.0,
                op0=AL.mult, op1=AL.mult))
            p.drain("vector")
            p.run("vector", lambda En, tt=tt: En.tensor_tensor(
                rr2_col.ap(tt, tt + 1), r_col.ap(tt, tt + 1), tcol.ap(8, 9), op=AL.mult))
            p.run("vector", lambda En, tt=tt: En.tensor_tensor(
                mrr2_col.ap(tt, tt + 1), mr_col.ap(tt, tt + 1), tcol.ap(8, 9), op=AL.mult))
            p.drain("vector")

            p.wait("vector", "t", stage_t)
            if tt >= 1:
                p.wait("vector", "sd", nrm_st_tok)
            p.run("vector", lambda En, tt=tt, par=par: En.tensor_scalar(
                xn_stage.ap(), xld[par].ap(), r_col.ap(tt, tt + 1), mr_col.ap(tt, tt + 1),
                op0=AL.mult, op1=AL.add))
            p.run("vector", lambda En, tt=tt, par=par: En.tensor_scalar(
                normed_pre.ap(), xld[par].ap(), rr2_col.ap(tt, tt + 1), mrr2_col.ap(tt, tt + 1),
                op0=AL.mult, op1=AL.add))
            stg_tok = p.run("vector", lambda En: En.tensor_tensor(
                normed_stage.ap(), normed_pre.ap(), nw_bc.ap(), op=AL.mult), inc="v", n=1)
            ln_v_done[tt] = stg_tok
            p.wait("sync", "v", stg_tok)
            nrm_st_tok = p.run("sync", lambda En, tt=tt: En.dma_start(
                out=nrm_d[tt * 128:(tt + 1) * 128, :], in_=normed_stage.ap()), inc="sd", n=16)

            p.wait("tensor", "v", stg_tok)
            for kind, stage, bank0 in (("xn", xn_stage, 0), ("nm", normed_stage, 2)):
                bank = bank0 + par
                if (kind, par) in evac_v:
                    p.wait("tensor", "v", evac_v[(kind, par)])
                pa_bf = PA[:, bank * 512:(bank + 1) * 512].bitcast(BF16)
                for c in range(NC):
                    inc = "t" if c == NC - 1 else None
                    ln_mm_t = p.run("tensor", lambda En, c=c, pa_bf=pa_bf, stage=stage: En.transpose(
                        pa_bf[:, c * 128:(c + 1) * 128], stage.ap(c * 128, (c + 1) * 128),
                        ident.ap()), inc=inc)
            stage_t = ln_mm_t

            p.wait("vector", "t", ln_mm_t)
            for kind, dstbuf, bank0 in (("xn", xn_fm, 0), ("nm", normed_fm, 2)):
                bank = bank0 + par
                pa_bf = PA[:, bank * 512:(bank + 1) * 512].bitcast(BF16)
                src3 = pa_bf.rearrange("p (c j) -> p c j", c=NC)
                dst3 = dstbuf.ap().rearrange("p (c t j) -> p c t j", c=NC, t=NT)[:, :, tt, :]
                evac_v[(kind, par)] = p.run("vector", lambda En, d=dst3, s=src3: En.tensor_copy(d, s),
                                            inc="v", n=1)
        ln_done_v = p.tok("v")

        wq_used_t = {}
        wq_tok = {}
        qk_evac_sc = {}
        last_t = 0

        def load_half(j, col0, buf):
            if j >= 2:
                p.wait("gpsimd", "t", wq_used_t.get(j - 2, 0))
            else:
                p.wait("gpsimd", "t", stage_t)
                p.wait("gpsimd", "v", ln_done_v)
                p.wait("gpsimd", "sc", p.tok("sc"))
            tokk = None
            for c in range(NC):
                tokk = p.run("gpsimd", lambda En, c=c, col0=col0, buf=buf: En.dma_start(
                    out=wq_buf[buf].ap(c * 512, (c + 1) * 512),
                    in_=wqkv_d[c * 128:(c + 1) * 128, col0:col0 + 512]), inc="gd", n=16)
            wq_tok[j] = tokk

        for j, (part, hf) in enumerate((("q", 0), ("q", 1), ("k", 0), ("k", 1))):
            col0 = (0 if part == "q" else D) + hf * 512
            buf = j % 2
            load_half(j, col0, buf)
            dstbuf = q_fm if part == "q" else k_fm
            bias_base = 0 if part == "q" else 8
            p.wait("tensor", "gd", wq_tok[j])
            p.wait("tensor", "v", ln_done_v)
            for ftl in range(4):
                chunk = hf * 4 + ftl
                for qb in range(4):
                    ppar = (ftl * 4 + qb) % 2
                    if ppar in qk_evac_sc:
                        p.wait("tensor", "sc", qk_evac_sc[ppar])
                    for c in range(NC):
                        inc = "t" if c == NC - 1 else None
                        last_t = p.run("tensor", lambda En, c=c, ftl=ftl, qb=qb, ppar=ppar, buf=buf: En.matmul(
                            PA[:, ppar * 512:(ppar + 1) * 512],
                            wq_buf[buf].ap(c * 512 + ftl * 128, c * 512 + ftl * 128 + 128),
                            xn_fm.ap(c * 2048 + qb * 512, c * 2048 + (qb + 1) * 512),
                            start=(c == 0), stop=(c == NC - 1)), inc=inc)
                    p.wait("scalar", "t", last_t)
                    bias = b_qkv_pp.ap(bias_base + chunk, bias_base + chunk + 1)
                    qk_evac_sc[ppar] = p.run("scalar", lambda En, ppar=ppar, chunk=chunk, qb=qb,
                                             dstbuf=dstbuf, bias=bias: En.activation(
                        dstbuf.ap(chunk * 2048 + qb * 512, chunk * 2048 + (qb + 1) * 512),
                        PA[:, ppar * 512:(ppar + 1) * 512], ACT.Identity, bias=bias, scale=1.0),
                        inc="sc", n=1)
            wq_used_t[j] = last_t

        slices = [(0, 512), (512, 1024), (1024, NR)]
        rope_prev_v = 0
        qk_sc_done = p.tok("sc")
        for srcbuf in (q_fm, k_fm):
            for c in range(NC):
                p.wait("tensor", "sc", qk_sc_done)
                p.wait("tensor", "v", max(rope_prev_v, consts_v))
                for si, (a, b) in enumerate(slices):
                    w = b - a
                    inc = "t" if si == 2 else None
                    rot_tok = p.run("tensor", lambda En, a=a, w=w, c=c, srcbuf=srcbuf: En.matmul(
                        PB[:, a:a + w], R2.ap(),
                        srcbuf.ap(c * 2048 + PRE + a, c * 2048 + PRE + a + w),
                        start=True, stop=True), inc=inc)
                p.wait("vector", "sc", qk_sc_done)
                p.wait("vector", "t", rot_tok)
                for si, (a, b) in enumerate(slices):
                    w = b - a
                    p.run("vector", lambda En, a=a, w=w, c=c, srcbuf=srcbuf: En.tensor_tensor(
                        rope_s1.ap(0, w), srcbuf.ap(c * 2048 + PRE + a, c * 2048 + PRE + a + w),
                        cos_sb.ap(a, a + w), op=AL.mult))
                    p.run("vector", lambda En, a=a, w=w: En.tensor_tensor(
                        rope_s2.ap(0, w), PB[:, a:a + w], sin_sb.ap(a, a + w), op=AL.mult))
                    inc = "v" if si == 2 else None
                    rope_prev_v = p.run("vector", lambda En, a=a, w=w, c=c, srcbuf=srcbuf: En.tensor_tensor(
                        srcbuf.ap(c * 2048 + PRE + a, c * 2048 + PRE + a + w),
                        rope_s1.ap(0, w), rope_s2.ap(0, w), op=AL.add), inc=inc)
        rope_done_v = rope_prev_v

        v_evac_v = {}
        for hf in range(2):
            j = 4 + hf
            load_half(j, 2 * D + hf * 512, j % 2)
            p.wait("tensor", "gd", wq_tok[j])
            p.wait("tensor", "v", rope_done_v)
            for tt in range(NT):
                par2 = (hf * NT + tt) % 2
                if par2 in v_evac_v:
                    p.wait("tensor", "v", v_evac_v[par2])
                for c in range(NC):
                    inc = "t" if c == NC - 1 else None
                    last_t = p.run("tensor", lambda En, c=c, tt=tt, par2=par2, buf=j % 2: En.matmul(
                        PB[:, par2 * 512:(par2 + 1) * 512],
                        xn_fm.ap(c * 2048 + tt * 128, c * 2048 + tt * 128 + 128),
                        wq_buf[buf].ap(c * 512, (c + 1) * 512),
                        start=(c == 0), stop=(c == NC - 1)), inc=inc)
                p.wait("vector", "t", last_t)
                dst3 = v_tm.ap().rearrange("p (t h j) -> p t h j", t=NT, h=H)[:, tt, hf * 8:(hf + 1) * 8, 0:64]
                src3 = PB[:, par2 * 512:(par2 + 1) * 512].rearrange("p (h j) -> p h j", h=8)
                bias3 = bias = b_v_bc.ap(hf * 512, (hf + 1) * 512).rearrange("p (h j) -> p h j", h=8)
                v_evac_v[par2] = p.run("vector", lambda En, d=dst3, s=src3, b3=bias3: En.tensor_tensor(
                    d, s, b3, op=AL.add), inc="v", n=1)
            wq_used_t[j] = last_t
        v_done_v = p.tok("v")

        router_v = 0
        p.wait("tensor", "gd", init_gd)
        for tt in range(NT):
            p.wait("tensor", "sc", qk_sc_done)
            p.wait("tensor", "v", router_v)
            for c in range(NC):
                inc = "t" if c == NC - 1 else None
                last_t = p.run("tensor", lambda En, c=c, tt=tt: En.matmul(
                    PA[:, 0:8],
                    normed_fm.ap(c * 2048 + tt * 128, c * 2048 + tt * 128 + 128),
                    rt_sb.ap(c * 8, (c + 1) * 8),
                    start=(c == 0), stop=(c == NC - 1)), inc=inc)
            p.wait("vector", "t", last_t)
            p.run("vector", lambda En: En.tensor_copy(lg_s.ap(), PA[:, 0:8]))
            p.drain("vector")
            p.run("vector", lambda En: En.max(mx8.ap(), lg_s.ap()))
            p.drain("vector")
            p.run("vector", lambda En: En.max_index(mi8.ap(), mx8.ap(), lg_s.ap()))
            p.drain("vector")
            p.run("vector", lambda En, tt=tt: En.tensor_copy(idx_cols.ap(tt, tt + 1), mi8.ap(0, 1)))
            p.run("vector", lambda En, tt=tt: En.tensor_copy(idx_cols.ap(NT + tt, NT + tt + 1), mi8.ap(1, 2)))
            p.drain("vector")
            d21_tok = p.runD("vector", lambda En: En.tensor_tensor(
                tcol.ap(9, 10), mx8.ap(1, 2), mx8.ap(0, 1), op=AL.subtract), inc="v", n=1)
            p.wait("scalar", "v", d21_tok)
            ee_tok = p.runD("scalar", lambda En: En.activation(
                tcol.ap(10, 11), tcol.ap(9, 10), ACT.Exp), inc="sc", n=1)
            p.wait("vector", "sc", ee_tok)
            p.run("vector", lambda En, tt=tt: En.tensor_scalar(
                m1m2.ap(0, 8), io8.ap(), idx_cols.ap(tt, tt + 1), None, op0=AL.is_equal))
            p.run("vector", lambda En, tt=tt: En.tensor_scalar(
                m1m2.ap(8, 16), io8.ap(), idx_cols.ap(NT + tt, NT + tt + 1), None, op0=AL.is_equal))
            p.run("vector", lambda En: En.tensor_scalar(tcol.ap(11, 12), tcol.ap(10, 11), 1.0, None,
                                                        op0=AL.add))
            p.drain("vector")
            p.run("vector", lambda En, tt=tt: En.reciprocal(wA_col.ap(tt, tt + 1), tcol.ap(11, 12)))
            p.drain("vector")
            p.run("vector", lambda En, tt=tt: En.tensor_scalar(
                wB_col.ap(tt, tt + 1), wA_col.ap(tt, tt + 1), -1.0, 1.0, op0=AL.mult, op1=AL.add))
            router_v = p.run("vector", lambda En, tt=tt: En.tensor_tensor(
                c_coef.ap(tt * 8, (tt + 1) * 8), m1m2.ap(0, 8), m1m2.ap(8, 16), op=AL.add),
                inc="v", n=1)
        router_done_v = router_v

        pv_free_v = [0, 0]
        exp_sc = {0: 0, 1: 0}
        pv_t = {}
        pT_used_t = [0, 0, 0, 0]
        attn_last_v = 0
        first_st = True
        NG = 8

        for cp in range(H // 2):
            for qb in range(4):
                st_t = {}

                def do_st(slot, g):
                    nonlocal first_st
                    h = 2 * cp + slot
                    r0 = 64 * slot
                    p.wait("tensor", "sc", exp_sc[slot])
                    if first_st:
                        p.wait("tensor", "v", router_done_v)
                        first_st = False
                    tok = 0
                    for kk in range(2):
                        kt = 2 * g + kk
                        bank = 2 * slot + kk
                        inc = "t" if kk == 1 else None
                        tok = p.run("tensor", lambda En, kk=kk, kt=kt, bank=bank, cp=cp, r0=r0, qb=qb: En.matmul(
                            PA[:, bank * 512:(bank + 1) * 512],
                            k_fm.ap(cp * 2048 + kt * 128, cp * 2048 + kt * 128 + 128, r0, r0 + 64),
                            q_fm.ap(cp * 2048 + qb * 512, cp * 2048 + (qb + 1) * 512, r0, r0 + 64),
                            start=True, stop=True), inc=inc)
                    st_t[(slot, g)] = tok

                def do_exp(slot, g):
                    buf = 2 * slot + (g % 2)
                    p.wait("scalar", "t", st_t[(slot, g)])
                    p.wait("scalar", "t", pT_used_t[buf])
                    exp_sc[slot] = p.run("scalar", lambda En, slot=slot, buf=buf: En.activation(
                        pT[buf].ap(), PA[:, slot * 1024: slot * 1024 + 1024],
                        ACT.Exp, bias=negm.ap(), scale=0.125), inc="sc", n=1)
                    exp_sc[(slot, g)] = exp_sc[slot]

                def do_pv(slot, g):
                    h = 2 * cp + slot
                    p.wait("tensor", "sc", exp_sc[(slot, g)])
                    if g == 0:
                        p.wait("tensor", "v", pv_free_v[slot])
                    buf = 2 * slot + (g % 2)
                    tok = 0
                    for kk in range(2):
                        for qt in range(4):
                            st_ = (g == 0 and kk == 0 and qt == 0)
                            sp_ = (g == NG - 1 and kk == 1 and qt == 3)
                            inc = "t" if (kk == 1 and qt == 3) else None
                            tok = p.run("tensor", lambda En, g=g, kk=kk, qt=qt, h=h, slot=slot,
                                        buf=buf, st_=st_, sp_=sp_: En.matmul(
                                PB[:, slot * 512 + qt * 65: slot * 512 + qt * 65 + 65],
                                pT[buf].ap(kk * 512 + qt * 128, kk * 512 + qt * 128 + 128),
                                v_tm.ap((2 * g + kk) * 1040 + h * 65, (2 * g + kk) * 1040 + (h + 1) * 65),
                                start=st_, stop=sp_, skip_group_check=True), inc=inc)
                    pv_t[(slot, g)] = tok
                    pT_used_t[buf] = tok

                for g in range(NG):
                    do_st(0, g)
                    do_st(1, g)
                    p.drain("tensor")
                    do_exp(0, g)
                    do_exp(1, g)
                    if g > 0:
                        do_pv(0, g - 1)
                        do_pv(1, g - 1)
                    p.drain("tensor")
                do_pv(0, NG - 1)
                do_pv(1, NG - 1)
                p.drain("tensor")

                for slot in range(2):
                    h = 2 * cp + slot
                    p.wait("vector", "t", pv_t[(slot, NG - 1)])
                    for qt in range(4):
                        p.run("vector", lambda En, qt=qt, slot=slot: En.reciprocal(
                            rec4.ap(qt, qt + 1), PB[:, slot * 512 + qt * 65 + 64: slot * 512 + qt * 65 + 65]))
                    p.drain("vector")
                    for qt in range(4):
                        gslot = qb * 4 + qt
                        inc = "v" if qt == 3 else None
                        attn_last_v = p.run("vector", lambda En, qt=qt, slot=slot, gslot=gslot, h=h: En.tensor_scalar(
                            O_tm.ap(gslot * 1024 + h * 64, gslot * 1024 + (h + 1) * 64),
                            PB[:, slot * 512 + qt * 65: slot * 512 + qt * 65 + 64],
                            rec4.ap(qt, qt + 1), None, op0=AL.mult), inc=inc)
                    pv_free_v[slot] = attn_last_v
        attn_done_v = p.tok("v")
        last_exp_sc = max(exp_sc[0], exp_sc[1])

        p.wait("gpsimd", "t", wq_used_t[5])
        p.wait("gpsimd", "v", rope_done_v)
        for c in range(NC):
            wproj_tok = p.run("gpsimd", lambda En, c=c: En.dma_start(
                out=wproj_buf.ap(c * 1024, (c + 1) * 1024),
                in_=wproj_d[c * 128:(c + 1) * 128, :]), inc="gd", n=16)

        otr_evac_v = {}
        otr_v = [0] * NT
        p.wait("tensor", "v", attn_done_v)
        p.wait("tensor", "sc", last_exp_sc)
        for g in range(NT):
            par = g % 2
            if par in otr_evac_v:
                p.wait("tensor", "v", otr_evac_v[par])
            pa_bf = PA[:, par * 512:(par + 1) * 512].bitcast(BF16)
            for c in range(NC):
                inc = "t" if c == NC - 1 else None
                otr_tok = p.run("tensor", lambda En, c=c, g=g, pa_bf=pa_bf: En.transpose(
                    pa_bf[:, c * 128:(c + 1) * 128],
                    O_tm.ap(g * 1024 + c * 128, g * 1024 + (c + 1) * 128), ident.ap()), inc=inc)
            p.wait("vector", "t", otr_tok)
            src3 = pa_bf.rearrange("p (c j) -> p c j", c=NC)
            dst3 = O_fm.ap(0, NC * T).rearrange("p (c t j) -> p c t j", c=NC, t=NT)[:, :, g, :]
            otr_v[g] = p.run("vector", lambda En, d=dst3, s=src3: En.tensor_copy(d, s), inc="v", n=1)
            otr_evac_v[par] = otr_v[g]

        proj_evac_v = {}
        p.wait("tensor", "gd", wproj_tok)
        for tt in range(NT):
            par = tt % 2
            p.wait("tensor", "v", otr_v[tt])
            if par in proj_evac_v:
                p.wait("tensor", "v", proj_evac_v[par])
            for c in range(NC):
                for dh2 in range(2):
                    inc = "t" if (c == NC - 1 and dh2 == 1) else None
                    proj_tok = p.run("tensor", lambda En, c=c, dh2=dh2, tt=tt, par=par: En.matmul(
                        PB[:, par * 1024 + dh2 * 512: par * 1024 + (dh2 + 1) * 512],
                        O_fm.ap(c * 2048 + tt * 128, c * 2048 + tt * 128 + 128),
                        wproj_buf.ap(c * 1024 + dh2 * 512, c * 1024 + (dh2 + 1) * 512),
                        start=(c == 0), stop=(c == NC - 1)), inc=inc)
            p.wait("vector", "t", proj_tok)
            proj_evac_v[par] = p.run("vector", lambda En, tt=tt, par=par: En.tensor_tensor(
                s_tm.ap(tt * 1024, (tt + 1) * 1024), PB[:, par * 1024:(par + 1) * 1024],
                b_proj_bc.ap(), op=AL.add), inc="v", n=1)
        proj_t_done = p.tok("t")

        colidx2 = r6(12288, 128, F32)
        triS = r6(12544, 128, F32)
        ones128 = r6(12800, 128, F32)
        sl_sc = r6(13120, 16, F32)
        au32 = r6(13152, 8, U32)
        S8 = r6(13168, 8, F32)
        msk8 = r6(13184, 16, F32)
        il_init = r6(15040, 48, U32)
        silu_e = [r6(13312, 768, BF16), r6(14080, 768, BF16)]
        tok_sb = r6(14848, 48, U32)
        dst_sb = r6(14944, 48, U32)

        p.wait("gpsimd", "v", rope_done_v)
        p.run("gpsimd", lambda En: En.iota(colidx2.ap(), [[1, 128]], channel_multiplier=0,
                                           allow_small_or_imprecise_dtypes=True), inc="gd", n=1)
        p.run("gpsimd", lambda En: En.iota(il_init.ap(), [[1, 48]], base=2 * T,
                                           channel_multiplier=48,
                                           allow_small_or_imprecise_dtypes=True), inc="gd", n=1)
        p.run("gpsimd", lambda En: En.dma_start(
            out=dstl_d[0:E * CAP].rearrange("(p k) -> p k", p=128), in_=il_init.ap()), inc="gd", n=16)
        p.run("gpsimd", lambda En: En.memset(il_init.ap(), 0), inc="gd", n=1)
        ini_tok = p.run("gpsimd", lambda En: En.dma_start(
            out=tokl_d[0:E * CAP].rearrange("(p k) -> p k", p=128), in_=il_init.ap()), inc="gd", n=16)
        p.wait("gpsimd", "gd", ini_tok)

        p.wait("vector", "gd", p.tok("gd"))
        p.run("vector", lambda En: En.memset(ones128.ap(), 1.0))
        rc_tok = p.runD("vector", lambda En: En.tensor_scalar(
            triS.ap(), colidx2.ap(), pidx.ap(0, 1), None, op0=AL.is_gt), inc="v", n=1)

        route_v = 0
        addr_gd = [0] * NT
        for tt in range(NT):
            p.wait("tensor", "v", max(rc_tok, router_done_v))
            p.wait("tensor", "v", route_v)
            for j in range(tt + 1):
                lhs = ones128 if j < tt else triS
                inc = "t" if j == tt else None
                pf_tok = p.run("tensor", lambda En, j=j, lhs=lhs: En.matmul(
                    PA[:, 0:8], lhs.ap(), c_coef.ap(j * 8, (j + 1) * 8),
                    start=(j == 0), stop=(j == tt)), inc=inc)
            p.wait("vector", "t", pf_tok)
            if tt >= 2:
                p.wait("vector", "gd", addr_gd[tt - 2])
            p.run("vector", lambda En: En.tensor_copy(S8.ap(), PA[:, 0:8]))
            p.drain("vector")
            for r in range(2):
                p.run("vector", lambda En, r=r, tt=tt: En.tensor_scalar(
                    msk8.ap(r * 8, r * 8 + 8), io8.ap(), idx_cols.ap(r * NT + tt, r * NT + tt + 1),
                    None, op0=AL.is_equal))
            p.drain("vector")
            for r in range(2):
                p.run("vector", lambda En, r=r: En.tensor_tensor(
                    msk8.ap(r * 8, r * 8 + 8), msk8.ap(r * 8, r * 8 + 8), S8.ap(), op=AL.mult))
            p.drain("vector")
            for r in range(2):
                p.run("vector", lambda En, r=r: En.reduce_sum(
                    sl_sc.ap(r, r + 1), msk8.ap(r * 8, r * 8 + 8), axis=mybir.AxisListType.X))
            p.drain("vector")
            for r in range(2):
                p.run("vector", lambda En, r=r: En.tensor_scalar_min(
                    sl_sc.ap(r, r + 1), sl_sc.ap(r, r + 1), float(CAP - 1)))
            p.drain("vector")
            for r in range(2):
                p.run("vector", lambda En, r=r: En.tensor_scalar(
                    sl_sc.ap(6 + r, 7 + r), sl_sc.ap(r, r + 1), 128.0, None, op0=AL.is_ge))
            for k in range(2, CT):
                for r in range(2):
                    p.run("vector", lambda En, r=r, k=k: En.tensor_scalar(
                        sl_sc.ap(8 + r, 9 + r), sl_sc.ap(r, r + 1), float(128 * k), None,
                        op0=AL.is_ge))
                p.drain("vector")
                for r in range(2):
                    p.run("vector", lambda En, r=r: En.tensor_tensor(
                        sl_sc.ap(6 + r, 7 + r), sl_sc.ap(6 + r, 7 + r), sl_sc.ap(8 + r, 9 + r),
                        op=AL.add))
                p.drain("vector")
            for r in range(2):
                p.run("vector", lambda En, r=r: En.tensor_scalar(
                    sl_sc.ap(10 + r, 11 + r), sl_sc.ap(6 + r, 7 + r), -128.0, None, op0=AL.mult))
            p.drain("vector")
            for r in range(2):
                p.run("vector", lambda En, r=r: En.tensor_tensor(
                    sl_sc.ap(10 + r, 11 + r), sl_sc.ap(r, r + 1), sl_sc.ap(10 + r, 11 + r),
                    op=AL.add))
            p.drain("vector")
            for r in range(2):
                p.run("vector", lambda En, r=r: En.tensor_scalar(
                    sl_sc.ap(2 + r, 3 + r), sl_sc.ap(10 + r, 11 + r), 48.0, None, op0=AL.mult))
                p.run("vector", lambda En, r=r, tt=tt: En.tensor_scalar(
                    sl_sc.ap(8 + r, 9 + r), idx_cols.ap(r * NT + tt, r * NT + tt + 1),
                    float(CT), None, op0=AL.mult))
            p.drain("vector")
            for r in range(2):
                p.run("vector", lambda En, r=r: En.tensor_tensor(
                    sl_sc.ap(2 + r, 3 + r), sl_sc.ap(2 + r, 3 + r), sl_sc.ap(8 + r, 9 + r),
                    op=AL.add))
            p.drain("vector")
            for r in range(2):
                p.run("vector", lambda En, r=r: En.tensor_tensor(
                    sl_sc.ap(2 + r, 3 + r), sl_sc.ap(2 + r, 3 + r), sl_sc.ap(6 + r, 7 + r),
                    op=AL.add))
            p.run("vector", lambda En, tt=tt: En.tensor_scalar(
                sl_sc.ap(4, 5), pidx.ap(0, 1), 1.0, float(tt * 128), op0=AL.mult, op1=AL.add))
            p.drain("vector")
            p.run("vector", lambda En: En.tensor_scalar(
                sl_sc.ap(5, 6), sl_sc.ap(4, 5), 1.0, float(T), op0=AL.mult, op1=AL.add))
            p.drain("vector")
            p.run("vector", lambda En: En.tensor_copy(au32.ap(0, 1), sl_sc.ap(2, 3)))
            p.run("vector", lambda En: En.tensor_copy(au32.ap(1, 2), sl_sc.ap(3, 4)))
            p.run("vector", lambda En: En.tensor_copy(au32.ap(2, 3), sl_sc.ap(4, 5)))
            route_v = p.runD("vector", lambda En: En.tensor_copy(au32.ap(3, 4), sl_sc.ap(5, 6)),
                             inc="v", n=1)
            p.wait("gpsimd", "v", route_v)
            p.run("gpsimd", lambda En: En.indirect_dma_start(
                out=tokl_d.rearrange("(a b) -> a b", b=1), out_offset=bass.IndirectOffsetOnAxis(au32.ap(0, 1), 0),
                in_=au32.ap(2, 3), in_offset=None), inc="gd", n=16)
            p.run("gpsimd", lambda En: En.indirect_dma_start(
                out=tokl_d.rearrange("(a b) -> a b", b=1), out_offset=bass.IndirectOffsetOnAxis(au32.ap(1, 2), 0),
                in_=au32.ap(2, 3), in_offset=None), inc="gd", n=16)
            p.run("gpsimd", lambda En: En.indirect_dma_start(
                out=dstl_d.rearrange("(a b) -> a b", b=1), out_offset=bass.IndirectOffsetOnAxis(au32.ap(0, 1), 0),
                in_=au32.ap(2, 3), in_offset=None), inc="gd", n=16)
            addr_gd[tt] = p.run("gpsimd", lambda En: En.indirect_dma_start(
                out=dstl_d.rearrange("(a b) -> a b", b=1), out_offset=bass.IndirectOffsetOnAxis(au32.ap(1, 2), 0),
                in_=au32.ap(3, 4), in_offset=None), inc="gd", n=16)
        p.wait("gpsimd", "gd", addr_gd[NT - 1])
        p.run("gpsimd", lambda En: En.dma_start(
            out=tok_sb.ap(), in_=tokl_d[0:E * CAP].rearrange("(p k) -> p k", p=128)), inc="gd", n=16)
        rb_tok = p.run("gpsimd", lambda En: En.dma_start(
            out=dst_sb.ap(), in_=dstl_d[0:E * CAP].rearrange("(p k) -> p k", p=128)), inc="gd", n=16)

        xe_tm = [Buf(arena, REG1.off + st * 1024, D, BF16) for st in range(CT)]
        out_e = [Buf(arena, REG1.off + 6144 + st * 1024, D, BF16) for st in range(CT)]
        gup_sp = [Buf(arena, REG1.off + 12288 + i * 1024, D, BF16) for i in range(4)]
        hid_sp = [Buf(arena, REG4.off + i * 768, CAP, BF16) for i in range(21)] + \
                 [Buf(arena, REG5.off, CAP, BF16)]
        xe_fm = Buf(arena, REG5.off + 768, NC * CAP, BF16)
        dn_sp = [Buf(arena, REG5.off + 6912 + k * 1024, D, BF16) for k in range(9)] + \
                [Buf(arena, REG6.off + (k - 9) * 1024, D, BF16) for k in range(9, HT)]

        gup_slot_t = [0] * 4
        dn_used_t = 0
        xe_tr_t = 0
        out_sc_gd = [0] * CT
        xfm_ev = {}
        sil_used = [0, 0]
        gup_free = [0, 0]
        hid_used_t = 0
        dn_ev = {}
        first_ffn_g = True
        nrm_sd_all = p.tok("sd")

        for e in range(E):
            if first_ffn_g:
                p.wait("gpsimd", "gd", rb_tok)
                p.wait("gpsimd", "sd", nrm_sd_all)
                p.wait("gpsimd", "t", otr_tok)
                p.wait("gpsimd", "v", attn_done_v)
            p.wait("gpsimd", "t", xe_tr_t)
            g_toks = []
            for st in range(CT):
                g_toks.append(p.run("gpsimd", lambda En, e=e, st=st: En.indirect_dma_start(
                    out=xe_tm[st].ap(), out_offset=None, in_=nrm_d,
                    in_offset=bass.IndirectOffsetOnAxis(tok_sb.ap(e * CT + st, e * CT + st + 1), 0)),
                    inc="gd", n=16))

            xfm_all = p.tok("v")
            for st in range(CT):
                par = st % 2
                p.wait("tensor", "v", route_v)
                p.wait("tensor", "gd", g_toks[st])
                if par in xfm_ev:
                    p.wait("tensor", "v", xfm_ev[par])
                pa_bf = PA[:, par * 512:(par + 1) * 512].bitcast(BF16)
                for c in range(NC):
                    inc = "t" if c == NC - 1 else None
                    xe_tr_t = p.run("tensor", lambda En, c=c, st=st, pa_bf=pa_bf: En.transpose(
                        pa_bf[:, c * 128:(c + 1) * 128], xe_tm[st].ap(c * 128, (c + 1) * 128),
                        ident.ap()), inc=inc)
                p.wait("vector", "t", xe_tr_t)
                dst3 = xe_fm.ap().rearrange("p (c j) -> p c j", c=NC)[:, :, st * 128:(st + 1) * 128]
                src3 = pa_bf.rearrange("p (c j) -> p c j", c=NC)
                xfm_ev[par] = p.run("vector", lambda En, d=dst3, s=src3: En.tensor_copy(d, s),
                                    inc="v", n=1)
            xfm_all = p.tok("v")

            def load_gup_sp(i):
                sg_, su_ = (2 * i) % 4, (2 * i + 1) % 4
                p.wait("gpsimd", "t", gup_slot_t[sg_])
                p.run("gpsimd", lambda En, e=e, i=i, sg_=sg_: En.dma_start(
                    out=gup_sp[sg_].ap(), in_=gup_d[e, i]), inc="gd", n=16)
                p.wait("gpsimd", "t", gup_slot_t[su_])
                return p.run("gpsimd", lambda En, e=e, i=i, su_=su_: En.dma_start(
                    out=gup_sp[su_].ap(), in_=gup_d[e, HT + i]), inc="gd", n=16)

            if first_ffn_g:
                p.wait("gpsimd", "t", proj_t_done)
                p.wait("gpsimd", "sc", last_exp_sc)
                first_ffn_g = False
            gu_toks = {}
            gu_toks[0] = load_gup_sp(0)
            gu_toks[1] = load_gup_sp(1)
            dn_toks = []
            for k in range(HT):
                p.wait("gpsimd", "t", dn_used_t)
                dn_toks.append(p.run("gpsimd", lambda En, e=e, k=k: En.dma_start(
                    out=dn_sp[k].ap(), in_=down_d[e, k]), inc="gd", n=16))

            PSET = [PA, PB]
            for i in range(HT):
                ps = PSET[i % 2]
                sg_, su_ = (2 * i) % 4, (2 * i + 1) % 4
                p.wait("tensor", "gd", gu_toks[i])
                p.wait("tensor", "v", max(xfm_all, gup_free[i % 2]))
                for half, (n0, n1) in enumerate(((0, 512), (512, 768))):
                    for c in range(NC):
                        inc = "t" if (half == 1 and c == NC - 1) else None
                        gmm = p.run("tensor", lambda En, c=c, n0=n0, n1=n1, ps=ps, sg_=sg_: En.matmul(
                            ps[:, n0:n1], gup_sp[sg_].ap(c * 128, (c + 1) * 128),
                            xe_fm.ap().rearrange("p (c j) -> p c j", c=NC)[:, c, n0:n1],
                            start=(c == 0), stop=(c == NC - 1)), inc=inc)
                for half, (n0, n1) in enumerate(((1024, 1536), (1536, 1792))):
                    for c in range(NC):
                        inc = "t" if (half == 1 and c == NC - 1) else None
                        umm = p.run("tensor", lambda En, c=c, n0=n0, n1=n1, ps=ps, su_=su_: En.matmul(
                            ps[:, n0:n1], gup_sp[su_].ap(c * 128, (c + 1) * 128),
                            xe_fm.ap().rearrange("p (c j) -> p c j", c=NC)[:, c, n0 - 1024:n1 - 1024],
                            start=(c == 0), stop=(c == NC - 1)), inc=inc)
                p.wait("scalar", "t", gmm)
                p.wait("scalar", "v", sil_used[i % 2])
                sil = p.run("scalar", lambda En, i=i, ps=ps: En.activation(
                    silu_e[i % 2].ap(), ps[:, 0:768], ACT.Silu), inc="sc", n=1)
                p.wait("vector", "sc", sil)
                p.wait("vector", "t", umm)
                if i < 2:
                    p.wait("vector", "t", hid_used_t)
                hv = p.run("vector", lambda En, i=i, ps=ps: En.tensor_tensor(
                    hid_sp[i].ap(), silu_e[i % 2].ap(), ps[:, 1024:1792], op=AL.mult), inc="v", n=1)
                sil_used[i % 2] = hv
                gup_free[i % 2] = hv
                gup_slot_t[sg_] = umm
                gup_slot_t[su_] = umm
                if i + 2 < HT:
                    gu_toks[i + 2] = load_gup_sp(i + 2)
            hid_all = p.tok("v")

            for st in range(CT):
                ps = PSET[st % 2]
                p.wait("tensor", "v", hid_all)
                p.wait("tensor", "gd", dn_toks[HT - 1])
                if st % 2 in dn_ev:
                    p.wait("tensor", "v", dn_ev[st % 2])
                for k in range(HT):
                    for dh2 in range(2):
                        st_ = (k == 0)
                        sp_ = (k == HT - 1)
                        inc = "t" if (k == HT - 1 and dh2 == 1) else None
                        dmm = p.run("tensor", lambda En, k=k, dh2=dh2, st=st, ps=ps,
                                    st_=st_, sp_=sp_: En.matmul(
                            ps[:, dh2 * 512:(dh2 + 1) * 512],
                            hid_sp[k].ap(st * 128, (st + 1) * 128),
                            dn_sp[k].ap(dh2 * 512, (dh2 + 1) * 512),
                            start=st_, stop=sp_), inc=inc)
                p.wait("vector", "t", dmm)
                p.wait("vector", "gd", out_sc_gd[st])
                dn_ev[st % 2] = p.run("vector", lambda En, st=st, ps=ps: En.tensor_copy(
                    out_e[st].ap(), ps[:, 0:1024]), inc="v", n=1)
                p.wait("gpsimd", "v", dn_ev[st % 2])
                out_sc_gd[st] = p.run("gpsimd", lambda En, e=e, st=st: En.indirect_dma_start(
                    out=moeab_d, out_offset=bass.IndirectOffsetOnAxis(
                        dst_sb.ap(e * CT + st, e * CT + st + 1), 0),
                    in_=out_e[st].ap(), in_offset=None), inc="gd", n=16)
            dn_used_t = dmm
            hid_used_t = dmm
        ffn_t_done = p.tok("t")
        ffn_sc_done = p.tok("sc")
        moe_gd_all = p.tok("gd")

        moab_t = [r6(12288 + k * 1024, D, BF16) for k in range(4)]
        fin_v = [0] * NT
        store_sd = [0] * NT
        p.wait("sync", "t", ffn_t_done)
        p.wait("sync", "sc", ffn_sc_done)
        p.wait("sync", "gd", moe_gd_all)
        p.wait("sync", "v", p.tok("v"))
        for tt in range(NT):
            par = tt % 2
            if tt >= 2:
                p.wait("sync", "v", fin_v[tt - 2])
            x_tok = p.run("sync", lambda En, tt=tt, par=par: En.dma_start(
                out=fx[par].ap(), in_=x_d[tt * 128:(tt + 1) * 128, :]), inc="sd", n=16)
            p.run("sync", lambda En, tt=tt, par=par: En.dma_start(
                out=moab_t[2 * par].ap(), in_=moeab_d[tt * 128:(tt + 1) * 128, :]), inc="sd", n=16)
            x_tok = p.run("sync", lambda En, tt=tt, par=par: En.dma_start(
                out=moab_t[2 * par + 1].ap(), in_=moeab_d[T + tt * 128:T + (tt + 1) * 128, :]),
                inc="sd", n=16)
            p.wait("vector", "sd", x_tok)
            p.run("vector", lambda En, tt=tt, par=par: En.tensor_scalar(
                fscr1.ap(), fx[par].ap(), r_col.ap(tt, tt + 1), mr_col.ap(tt, tt + 1),
                op0=AL.mult, op1=AL.add))
            p.run("vector", lambda En, tt=tt, par=par: En.tensor_scalar(
                fscr2.ap(), moab_t[2 * par].ap(), wA_col.ap(tt, tt + 1), None, op0=AL.mult))
            p.drain("vector")
            p.run("vector", lambda En, tt=tt, par=par: En.scalar_tensor_tensor(
                fscr2.ap(), moab_t[2 * par + 1].ap(), wB_col.ap(tt, tt + 1), fscr2.ap(),
                op0=AL.mult, op1=AL.add))
            p.drain("vector")
            ffb = p.run("vector", lambda En: En.scalar_tensor_tensor(
                fscr2.ap(), fscr2.ap(), alpha_s.ap(), fscr1.ap(),
                op0=AL.mult, op1=AL.add), inc="v", n=1)
            p.wait("scalar", "v", ffb)
            g_tok = p.run("scalar", lambda En: En.activation(
                fscr1.ap(), fscr2.ap(), ACT.Silu), inc="sc", n=1)
            p.wait("vector", "sc", g_tok)
            p.run("vector", lambda En, tt=tt: En.tensor_tensor(
                fscr2.ap(), s_tm.ap(tt * 1024, (tt + 1) * 1024), fscr1.ap(), op=AL.mult))
            if tt >= 2:
                p.wait("vector", "sd", store_sd[tt - 2])
            fin_v[tt] = p.run("vector", lambda En, par=par: En.tensor_tensor(
                fout[par].ap(), fx[par].ap(), fscr2.ap(), op=AL.add), inc="v", n=1)
            p.wait("sync", "v", fin_v[tt])
            store_sd[tt] = p.run("sync", lambda En, tt=tt, par=par: En.dma_start(
                out=y_d[tt * 128:(tt + 1) * 128, :], in_=fout[par].ap()), inc="sd", n=16)


        p.wait("sync", "sd", p.tok("sd"))

        @block.tensor
        def _(En):
            for f in p.prog["tensor"]:
                f(En)

        @block.vector
        def _(En):
            for f in p.prog["vector"]:
                f(En)

        @block.scalar
        def _(En):
            for f in p.prog["scalar"]:
                f(En)

        @block.sync
        def _(En):
            for f in p.prog["sync"]:
                f(En)

        @block.gpsimd
        def _(En):
            for f in p.prog["gpsimd"]:
                f(En)

    return nc


_NC_CACHE = None


def kernel(**inputs):
    global _NC_CACHE
    x = np.ascontiguousarray(np.asarray(inputs["L0_attn_x"], dtype=np.float32))
    sin = np.asarray(inputs["L0_attn_sin"], dtype=np.float32)
    cos = np.asarray(inputs["L0_attn_cos"], dtype=np.float32)
    w_qkv = np.ascontiguousarray(np.asarray(inputs["L0_attn_w_qkv"], dtype=np.float32))
    b_qkv = np.asarray(inputs["L0_attn_b_qkv"], dtype=np.float32)
    w_proj = np.ascontiguousarray(np.asarray(inputs["L0_attn_w_proj"], dtype=np.float32))
    b_proj = np.asarray(inputs["L0_attn_b_proj"], dtype=np.float32)
    nw = np.asarray(inputs["L0_ffn_norm_weight"], dtype=np.float32)
    rw = np.asarray(inputs["L0_ffn_router_weight"], dtype=np.float32)
    gup_w = np.asarray(inputs["L0_ffn_gate_up_weight"], dtype=np.float32)
    down_w = np.asarray(inputs["L0_ffn_down_weight"], dtype=np.float32)
    alpha = np.asarray(inputs["L0_ffn_residual_alpha"], dtype=np.float32)

    sin_t = np.ascontiguousarray(np.tile(sin.T, (2, 1)))
    cos_t = np.ascontiguousarray(np.tile(cos.T, (2, 1)))
    b_qkv_pp = np.ascontiguousarray(b_qkv.reshape(24, 128).T)
    b_v_bc = np.ascontiguousarray(np.tile(b_qkv[2 * D:3 * D][None, :], (128, 1)))
    b_proj_bc = np.ascontiguousarray(np.tile(b_proj[None, :], (128, 1)))
    nw_bc = np.ascontiguousarray(np.tile(nw[None, :], (128, 1)))
    rt4 = np.ascontiguousarray(rw.T.reshape(NC, 128, E))
    gup = np.ascontiguousarray(
        gup_w.reshape(E, 2 * HT, 128, NC, 128).transpose(0, 1, 4, 3, 2).reshape(E, 2 * HT, 128, D))
    down = np.ascontiguousarray(
        down_w.transpose(0, 2, 1).reshape(E, HT, 128, D))
    alpha_t = np.ascontiguousarray(np.tile(alpha.reshape(1, 1), (128, 1)))

    if _NC_CACHE is None:
        _NC_CACHE = build_nc()
    nc = _NC_CACHE

    shared = dict(sin_t=sin_t, cos_t=cos_t, w_qkv=w_qkv, b_qkv_pp=b_qkv_pp,
                  b_v_bc=b_v_bc, w_proj=w_proj, b_proj_bc=b_proj_bc, nw_bc=nw_bc,
                  rt4=rt4, gup=gup, down=down, alpha_t=alpha_t)
    in_maps = [dict(x=np.ascontiguousarray(x[b]), **shared) for b in range(8)]
    res = run_bass_kernel_spmd(nc, in_maps, core_ids=list(range(8)))
    return np.stack([r["y"] for r in res.results], axis=0)
